# version 05
# baseline (speedup 1.0000x reference)
import math
from contextlib import ExitStack

import numpy as np

import concourse.bass as bass
import concourse.mybir as mybir
from concourse.bass_utils import run_bass_kernel_spmd

F32 = mybir.dt.float32
BF16 = mybir.dt.bfloat16
FP16 = mybir.dt.float16
AF = mybir.ActivationFunctionType
ALU = mybir.AluOpType
AX = mybir.AxisListType

DIM = 2048
HID = 4096
NH = 64
PH = 64
NST = 128
KC = 4
Q = 128
CONV = HID + 2 * NST
NPC = CONV // 128
TH = 1024
NT = TH // 128
NCH = 8
EPS_LN = 1e-5
EPS_RMS = 1e-5

ENGS = ("sync", "gpsimd", "tensor", "vector", "scalar")


class Bld:
    def __init__(self, nc, debug=()):
        self.nc = nc
        self.debug = set(debug)
        self.steps = {e: [] for e in ENGS}
        self.semh = {}
        self.cnt = {}
        self.seen = {e: {} for e in ENGS}
        self._stack = None
        self.tag = "init"
        self.inst_tags = {}

    def mksem(self, name):
        self.semh[name] = self._stack.enter_context(self.nc.semaphore(name))
        self.cnt[name] = 0

    def _wait(self, eng_name, ev):
        if ev is None:
            return
        if isinstance(ev, list):
            for e in ev:
                self._wait(eng_name, e)
            return
        sem, val = ev
        if val <= 0:
            return
        if self.seen[eng_name].get(sem, 0) >= val:
            return
        self.seen[eng_name][sem] = val
        self.steps[eng_name].append(("w", sem, val))

    def ev(self, sem):
        return (sem, self.cnt[sem])

    OWN = {"vector": "pv", "scalar": "ps", "gpsimd": "pg",
           "tensor": "pt", "sync": "psy"}
    AUTO = {"vector", "scalar", "gpsimd"}

    def op(self, eng_name, fn, wait=None, inc=None, inc_by=1, auto=None):
        self._wait(eng_name, wait)
        self._lasttag = self.tag
        own = self.OWN[eng_name]
        auto = (eng_name in self.AUTO) if auto is None else auto
        if auto:
            if self.cnt[own] > 0:
                self.steps[eng_name].append(("w", own, self.cnt[own]))
                self.seen[eng_name][own] = max(
                    self.seen[eng_name].get(own, 0), self.cnt[own])
            if inc is None or inc == own:
                self.steps[eng_name].append(("i", fn, own, 1, self.tag))
                self.cnt[own] += 1
                return (own, self.cnt[own])
            self.steps[eng_name].append(("i2", fn, own, inc, inc_by, self.tag))
            self.cnt[own] += 1
            self.cnt[inc] += inc_by
            return (inc, self.cnt[inc])
        self.steps[eng_name].append(("i", fn, inc, inc_by, self.tag))
        if inc is not None:
            self.cnt[inc] += inc_by
            return (inc, self.cnt[inc])
        return None

    def sy(self, fn, wait=None, inc=None):
        return self.op("sync", fn, wait, inc)

    def g(self, fn, wait=None, inc=None):
        return self.op("gpsimd", fn, wait, inc)

    def t(self, fn, wait=None, inc=None):
        return self.op("tensor", fn, wait, inc)

    def v(self, fn, wait=None, inc=None):
        return self.op("vector", fn, wait, inc)

    def s(self, fn, wait=None, inc=None):
        return self.op("scalar", fn, wait, inc)

    def dma(self, eng_name, out, in_, wait=None, sem="d"):
        self._wait(eng_name, wait)

        def fn(eng, out=out, in_=in_):
            return eng.dma_start(out=out, in_=in_)

        self.steps[eng_name].append(("i", fn, sem, 16, self.tag))
        self.cnt[sem] += 16
        return (sem, self.cnt[sem])

    def barrier(self, tag=""):
        evs = []
        for e, sname in (("sync", "psy"), ("gpsimd", "pg"), ("tensor", "pt"),
                         ("vector", "pv"), ("scalar", "ps")):
            self.steps[e].append(("w", sname, self.cnt[sname]))
            evs.append(self.op(
                e, lambda eng: (eng.engine_nop() if hasattr(eng, "engine_nop")
                                else eng.nop()), inc=sname, auto=False))
        for e in ENGS:
            self._wait(e, evs)
            for sname in self.cnt:
                if sname not in ("psy", "pg", "pt", "pv", "ps"):
                    self._wait(e, (sname, self.cnt[sname]))

    def finalize(self):
        nc = self.nc
        semh = self.semh

        def run(eng, eng_name):
            for si, st in enumerate(self.steps[eng_name]):
                if st[0] == "w":
                    eng.wait_ge(semh[st[1]], st[2])
                elif st[0] == "i2":
                    _, fn, own, inc, inc_by, tag = st
                    ins = fn(eng)
                    self.inst_tags[ins.ins.name] = (tag, eng_name, si)
                    ins.then_inc(semh[own], 1)
                    ins.then_inc(semh[inc], inc_by)
                else:
                    _, fn, inc, inc_by, tag = st
                    ins = fn(eng)
                    if ins is not None and hasattr(ins, "ins"):
                        self.inst_tags[ins.ins.name] = (tag, eng_name, si)
                    if inc is not None:
                        assert ins is not None
                        ins.then_inc(semh[inc], inc_by)

        blk = self._stack.enter_context(nc.Block())

        @blk.sync
        def _(eng):
            run(eng, "sync")

        @blk.gpsimd
        def _(eng):
            run(eng, "gpsimd")

        @blk.tensor
        def _(eng):
            run(eng, "tensor")

        @blk.vector
        def _(eng):
            run(eng, "vector")

        @blk.scalar
        def _(eng):
            run(eng, "scalar")


class Alloc:

    def __init__(self, ap, n, name):
        self.ap = ap
        self.n = n
        self.name = name
        self.regions = {}

    def reg(self, name, n_elems):
        if name in self.regions:
            off0, n0 = self.regions[name]
            assert n0 == n_elems, (name, n0, n_elems)
            return self.ap[:, off0:off0 + n_elems]
        taken = sorted(self.regions.values())
        off = 0
        for o, n in taken:
            if off + n_elems <= o:
                break
            off = max(off, o + n)
        assert off + n_elems <= self.n, \
            f"{self.name} overflow: {name} needs {n_elems} at {off} (cap {self.n})"
        self.regions[name] = (off, n_elems)
        return self.ap[:, off:off + n_elems]

    def at(self, name):
        off, n = self.regions[name]
        return self.ap[:, off:off + n]

    def free(self, *names):
        for nm in names:
            self.regions.pop(nm, None)


class _StopSched(Exception):
    pass


def build_program(debug=(), upto=None, skips=()):
    nc = bass.Bass("TRN2", target_bir_lowering=False)
    b = Bld(nc, debug=debug)
    stack = ExitStack()
    b._stack = stack

    dx = {}

    def din(name, shape, dtype=F32):
        dx[name] = nc.dram_tensor(name, list(shape), dtype,
                                  kind="ExternalInput").ap()
        return dx[name]

    din("x_own", [TH, DIM])
    din("x_halo", [TH, DIM])
    din("wz_t", [128, 16, HID])
    din("wbc_t", [128, 16, CONV])
    din("wdt_t", [128, 16, NH])
    din("wout_t", [128, 32, DIM])
    din("w1_t", [128, 16, 8192])
    din("w2_t", [128, 64, DIM])
    din("conv_w", [128, NPC * KC])
    din("dt_bias", [1, NH])
    din("a_log", [1, NH])
    din("dvec", [1, NH])
    din("norm_w", [128, 32])
    din("uincl", [128, 128])
    din("ustrict", [128, 128])
    din("eye", [128, 128])
    dout = nc.dram_tensor("out", [TH, DIM], F32, kind="ExternalOutput").ap()
    d_silz = nc.dram_tensor("silz_s", [NCH, 128, HID], BF16, kind="Internal").ap()
    d_sum = nc.dram_tensor("sum_s", [16, 128, TH], BF16, kind="Internal").ap()

    dbg_aps = {}

    NBF = 72 * 1024
    NF = 14 * 1024
    abf = stack.enter_context(nc.sbuf_tensor("abf", [128, NBF], BF16))[:]
    af = stack.enter_context(nc.sbuf_tensor("af", [128, NF], F32))[:]
    a16 = stack.enter_context(nc.sbuf_tensor("a16", [128, 2560], FP16))[:]
    r_u16x = a16[:, 0:2048]
    r_u16h = a16[:, 2048:2560]
    BF = Alloc(abf, NBF, "bf16")
    FL = Alloc(af, NF, "f32")

    r_uincl_bf = BF.reg("uincl_bf", 128)
    r_ustrict_bf = BF.reg("ustrict_bf", 128)
    r_eye_bf = BF.reg("eye_bf", 128)
    r_onescol_bf = BF.reg("onescol_bf", 1)
    r_S = BF.reg("S", HID)

    r_uincl = FL.reg("uincl", 128)
    r_lstrict = FL.reg("lstrict", 128)
    r_eye_f = FL.reg("eye_f", 128)
    r_onescol = FL.reg("onescol", 1)
    r_ones1x = FL.reg("ones1x", 128)
    r_convw = FL.reg("convw", NPC * KC)
    r_normw = FL.reg("normw", 32)
    r_abc = FL.reg("abc", NH)
    r_dbc = FL.reg("dbc", NH)
    r_dtbbc = FL.reg("dtbbc", NH)
    r_cone = FL.reg("c_one", 1)
    r_ceps = FL.reg("c_eps", 1)
    r_cepsr = FL.reg("c_epsr", 1)
    r_cinvh = FL.reg("c_invh", 1)
    r_row = FL.reg("row64", 3 * NH)
    r_mcol = FL.reg("mcol", NT)
    r_rscol = FL.reg("rscol", NT)
    r_nmrs = FL.reg("nmrs", NT)
    r_amax = FL.reg("amax", NT)
    r_var = FL.reg("var", NT)
    r_beta = FL.reg("beta", NT)
    r_dq = FL.reg("dq", NT)
    r_nd384 = FL.reg("nd384", NT)
    r_wsum = FL.reg("wsum", 132)
    r_thr1 = FL.reg("thr1", 4)
    r_m12 = FL.reg("m12", 2)
    r_dtok = FL.reg("dtok", NT * NH)
    r_drow = FL.reg("drow", 512)
    r_gt = FL.reg("gt", 128)
    r_rn = FL.reg("rn", 8)

    pb = [stack.enter_context(nc.psum_tensor(f"pb{i}", [128, 512], F32))[:]
          for i in range(6)]
    pT = [stack.enter_context(nc.psum_tensor(f"pT{i}", [128, 128], BF16))[:]
          for i in range(2)]

    for sname in ("d", "dg", "gw0", "gw1", "dw0", "dw1", "dx0", "dx1",
                  "dsz0", "dsz1", "dzw0", "dzw1", "daw0", "daw1", "dar", "dar1",
                  "dout0", "dout1", "psy", "pg", "pt", "pv", "ps"):
        b.mksem(sname)

    def ckpt(name):
        if upto is not None and name == upto:
            raise _StopSched()

    def dump(name, ap, wait=None):
        if name not in b.debug:
            return
        sh = list(ap.shape)
        d = nc.dram_tensor("dbg_" + name, sh, ap.dtype,
                           kind="ExternalOutput").ap()
        dbg_aps[name] = d
        b.dma("sync", d, ap, wait=wait, sem="d")
        for e in ENGS:
            b._wait(e, ("d", b.cnt["d"]))

    try:
        b.dma("sync", r_uincl, dx["uincl"], sem="d")
        b.dma("sync", r_lstrict, dx["ustrict"], sem="d")
        b.dma("sync", r_eye_f, dx["eye"], sem="d")
        b.dma("sync", r_convw, dx["conv_w"], sem="d")
        b.dma("sync", r_normw, dx["norm_w"], sem="d")
        b.dma("gpsimd", r_uincl_bf, dx["uincl"], sem="dg")
        b.dma("gpsimd", r_ustrict_bf, dx["ustrict"], sem="dg")
        b.dma("gpsimd", r_eye_bf, dx["eye"], sem="dg")
        roff = FL.regions["row64"][0]
        b.dma("sync", af[0:1, roff:roff + NH], dx["dt_bias"], sem="d")
        b.dma("sync", af[0:1, roff + NH:roff + 2 * NH], dx["a_log"], sem="d")
        b.dma("sync", af[0:1, roff + 2 * NH:roff + 3 * NH], dx["dvec"], sem="d")

        b.v(lambda v: v.memset(r_onescol, 1.0), inc="pv")
        b.v(lambda v: v.memset(r_ones1x[0:1, :], 1.0), inc="pv")
        b.v(lambda v: v.memset(r_onescol_bf, 1.0), inc="pv")
        b.v(lambda v: v.memset(r_cone, 1.0), inc="pv")
        b.v(lambda v: v.memset(r_ceps, EPS_LN), inc="pv")
        b.v(lambda v: v.memset(r_cepsr, EPS_RMS), inc="pv")
        b.v(lambda v: v.memset(r_cinvh, 1.0 / HID), inc="pv")
        ev_vm = b.v(lambda v: v.memset(r_S, 0.0), inc="pv")

        evm = b.t(lambda t: t.matmul(pb[5][:, 0:NH], r_ones1x[0:1, :],
                                     af[0:1, roff:roff + NH], start=True, stop=True),
                  wait=[b.ev("d"), b.ev("dg"), ev_vm], inc="pt")
        ev1 = b.v(lambda v: v.tensor_copy(r_dtbbc, pb[5][:, 0:NH]), wait=evm, inc="pv")
        evm = b.t(lambda t: t.matmul(pb[5][:, 0:NH], r_ones1x[0:1, :],
                                     af[0:1, roff + NH:roff + 2 * NH],
                                     start=True, stop=True), wait=ev1, inc="pt")
        ev2 = b.s(lambda s: s.activation(r_abc, pb[5][:, 0:NH], AF.Exp),
                  wait=evm, inc="ps")
        ev2b = b.v(lambda v: v.tensor_scalar(r_abc, r_abc, -1.0, None, ALU.mult),
                   wait=ev2, inc="pv")
        evm = b.t(lambda t: t.matmul(pb[5][:, 0:NH], r_ones1x[0:1, :],
                                     af[0:1, roff + 2 * NH:roff + 3 * NH],
                                     start=True, stop=True), wait=ev2b, inc="pt")
        b.v(lambda v: v.tensor_copy(r_dbc, pb[5][:, 0:NH]), wait=evm, inc="pv")
        b.barrier("consts")

        wab = BF.reg("wabs_sc", 2048)
        wfl_ab = FL.reg("wabs_f", 2 * 2048)
        wflv_ab = wfl_ab.rearrange("p (b n) -> p b n", b=2)
        _wst = {"ev_use": [None, None], "ncol": 0}

        def _wabs_step(i):
            pp = i % 2
            ncol = _wst["ncol"]
            if i < 64:
                k, hcol = i // 4, (i % 4) * 2048
                src = dx["w1_t"][:, k:k + 1, hcol:hcol + 2048]
            else:
                j = i - 64
                src = dx["w2_t"][:, j:j + 1, :]
            evl = b.dma("sync", wflv_ab[:, pp, :], src, wait=_wst["ev_use"][pp],
                        sem="dw0" if pp == 0 else "dw1")
            _wst["ev_use"][pp] = b.s(lambda s, pp=pp, ncol=ncol: s.activation(
                wab, wflv_ab[:, pp, :], AF.Abs,
                accum_out=r_wsum[:, ncol:ncol + 1]), wait=evl, inc="ps")
            _wst["ncol"] = ncol + 1

        wabs_steps = [(lambda i=i: _wabs_step(i)) for i in range(128)]

        def wabs_pump(n=1):
            for _ in range(n):
                if wabs_steps:
                    wabs_steps.pop(0)()

        def wabs_reduce():
            ev_abs = [e for e in _wst["ev_use"] if e is not None]
            ev_r1 = b.v(lambda v: v.reduce_sum(
                r_wsum[:, 128:129], r_wsum[:, 0:64], AX.X),
                wait=ev_abs, inc="pv")
            ev_r2 = b.v(lambda v: v.reduce_sum(
                r_wsum[:, 129:130], r_wsum[:, 64:128], AX.X),
                wait=ev_abs, inc="pv")
            evm1 = b.t(lambda t: t.matmul(pb[5][0:1, 0:1], r_wsum[:, 128:129],
                                          r_onescol, start=True, stop=True),
                       wait=ev_r1, inc="pt")
            woff = FL.regions["wsum"][0]
            ev_s1 = b.v(lambda v: v.tensor_scalar(
                af[0:1, woff + 130:woff + 131],
                pb[5][0:1, 0:1], 1.0 / (8192.0 * DIM), 1e-5, ALU.mult, ALU.max),
                wait=evm1, inc="pv")
            evm2 = b.t(lambda t: t.matmul(pb[5][0:1, 1:2], r_wsum[:, 129:130],
                                          r_onescol, start=True, stop=True),
                       wait=[ev_r2, ev_s1], inc="pt")
            ev_s2 = b.v(lambda v: v.tensor_scalar(
                af[0:1, woff + 131:woff + 132],
                pb[5][0:1, 1:2], 1.0 / (8192.0 * DIM), 1e-5, ALU.mult, ALU.max),
                wait=evm2, inc="pv")
            evb = b.t(lambda t: t.matmul(pb[5][:, 2:4], r_ones1x[0:1, :],
                                         af[0:1, woff + 130:woff + 132],
                                         start=True, stop=True),
                      wait=ev_s2, inc="pt")
            b.v(lambda v: v.tensor_copy(r_m12, pb[5][:, 2:4]), wait=evb, inc="pv")
            b.v(lambda v: v.tensor_scalar(r_thr1[:, 0:1], r_m12[:, 0:1], 0.5,
                                          None, ALU.mult), inc="pv")
            b.v(lambda v: v.tensor_scalar(r_thr1[:, 1:2], r_m12[:, 0:1], -0.5,
                                          None, ALU.mult), inc="pv")
            b.v(lambda v: v.tensor_scalar(r_thr1[:, 2:3], r_m12[:, 1:2], 0.5,
                                          None, ALU.mult), inc="pv")
            b.v(lambda v: v.tensor_scalar(r_thr1[:, 3:4], r_m12[:, 1:2], -0.5,
                                          None, ALU.mult), inc="pv")
            BF.free("wabs_sc")
            FL.free("wabs_f")

        _tb_ev = {}

        def pe_T(dst_ap, src_ap, bf=True, bank=0, wait=None, scale_col=None,
                 add_ap=None, out_f32=False, eng="v"):
            key = ("b" if bf else "f", bank)
            ident = r_eye_bf if bf else r_eye_f
            pbank = pT[bank] if bf else pb[4 + bank][:, 0:128]
            w = [] if wait is None else [wait]
            if _tb_ev.get(key) is not None:
                w.append(_tb_ev[key])
            evt = b.t(lambda t: t.transpose(pbank, src_ap, ident), wait=w, inc="pt")
            if add_ap is not None:
                eve = b.v(lambda v: v.scalar_tensor_tensor(
                    dst_ap, pbank, 1.0, add_ap, ALU.mult, ALU.add),
                    wait=evt, inc="pv")
            elif eng == "s":
                if scale_col is not None:
                    eve = b.s(lambda s: s.activation(dst_ap, pbank, AF.Identity,
                                                     scale=scale_col),
                              wait=evt, inc="ps")
                else:
                    eve = b.s(lambda s: s.activation(dst_ap, pbank, AF.Identity),
                              wait=evt, inc="ps")
            elif scale_col is not None:
                eve = b.v(lambda v: v.tensor_scalar(dst_ap, pbank, scale_col,
                                                    None, ALU.mult),
                          wait=evt, inc="pv")
            else:
                eve = b.v(lambda v: v.tensor_copy(dst_ap, pbank), wait=evt, inc="pv")
            _tb_ev[key] = eve
            return eve

        def run_phase(ph):
            is_m = ph == "M"
            xdram = dx["x_own"] if is_m else dx["x_halo"]

            b.tag = f"ln{ph}"
            xnT_r = BF.reg("xnT", 16 * TH)
            xnT = xnT_r.rearrange("p (k n) -> p k n", k=16)
            xt2 = FL.reg("xtile", 2 * DIM).rearrange("p (b n) -> p b n", b=2)
            xnsc = BF.reg("xnt_sc", 2 * DIM).rearrange("p (b n) -> p b n", b=2)
            sqsc = BF.reg("sq_sc", DIM)
            colt = FL.reg("colt", 8)
            ev_xt = [None, None]
            ev_xn = [None, None]
            for t0 in range(NT):
                pp = t0 % 2
                xt = xt2[:, pp, :]
                xnt = xnsc[:, pp, :]
                evl = b.dma("sync", xt, xdram[t0 * 128:(t0 + 1) * 128, :],
                            wait=ev_xt[pp], sem="dx0" if pp == 0 else "dx1")
                evq = b.s(lambda s, xt=xt: s.activation(
                    sqsc, xt, AF.Square, accum_out=colt[:, 0:1]),
                    wait=evl, inc="ps")
                evsum = b.v(lambda v, xt=xt: v.reduce_sum(colt[:, 1:2], xt, AX.X),
                            wait=evl, inc="pv")
                mcol = r_mcol[:, t0:t0 + 1]
                vcol = r_var[:, t0:t0 + 1]
                rscol = r_rscol[:, t0:t0 + 1]
                nmrs = r_nmrs[:, t0:t0 + 1]
                b.v(lambda v, mcol=mcol: v.tensor_scalar(
                    mcol, colt[:, 1:2], 1.0 / DIM, None, ALU.mult), wait=evsum)
                b.v(lambda v, mcol=mcol: v.tensor_tensor(
                    colt[:, 2:3], mcol, mcol, ALU.mult), wait=evq)
                b.v(lambda v: v.tensor_scalar(
                    colt[:, 3:4], colt[:, 0:1], 1.0 / DIM, None, ALU.mult))
                b.v(lambda v, vcol=vcol: v.tensor_tensor(
                    vcol, colt[:, 3:4], colt[:, 2:3], ALU.subtract))
                evsq = b.s(lambda s, vcol=vcol: s.activation(
                    colt[:, 4:5], vcol, AF.Sqrt, bias=r_ceps, scale=1.0),
                    wait=b.ev("pv"), inc="ps")
                evrs = b.v(lambda v, rscol=rscol: v.reciprocal(
                    rscol, colt[:, 4:5]), wait=evsq, inc="pv")
                evn = b.v(lambda v, mcol=mcol: v.tensor_scalar(
                    colt[:, 5:6], mcol, -1.0, None, ALU.mult), wait=evrs, inc="pv")
                evn2 = b.v(lambda v, nmrs=nmrs, rscol=rscol: v.tensor_tensor(
                    nmrs, colt[:, 5:6], rscol, ALU.mult), wait=evn, inc="pv")
                evxn = b.s(lambda s, xnt=xnt, xt=xt, rscol=rscol, nmrs=nmrs:
                           s.activation(xnt, xt, AF.Identity, bias=nmrs, scale=rscol),
                           wait=evn2, inc="ps")
                ev_xt[pp] = [evxn, evsum]
                if is_m:
                    b.v(lambda v, xnt=xnt, t0=t0: v.reduce_max(
                        r_amax[:, t0:t0 + 1], xnt, AX.X,
                        apply_absolute_value=True), wait=evxn, inc="pv")
                for k in range(16):
                    pe_T(xnT[:, k, t0 * 128:(t0 + 1) * 128],
                         xnt[:, k * 128:(k + 1) * 128],
                         bf=True, bank=k % 2, wait=evxn)
                ev_xn[pp] = b.ev("pv")
                ev_xt[pp].append(ev_xn[pp])
                wabs_pump(2)
            FL.free("xtile")
            BF.free("xnt_sc", "sq_sc")
            FL.free("colt")
            dump(f"xnT_{ph}", xnT_r, wait=[b.ev("pv")])
            b.barrier(f"ln{ph}")
            ckpt(f"ln{ph}")

            b.tag = "zstream"
            if is_m:
                wfl0dbg = FL.reg("zdbg", 256) if "zps0" in b.debug else None
                wzpan = [BF.reg("wpA", 8192).rearrange("p (k n) -> p k n", k=16),
                        BF.reg("wpB", 8192).rearrange("p (k n) -> p k n", k=16)]
                ztmp = [BF.reg("ztA", 512), BF.reg("ztB", 512)]
                ev_wu = [None, None]
                ev_zt = [None, None]
                for p in range(8):
                    pp = p % 2
                    evw = b.dma("gpsimd", wzpan[pp],
                                dx["wz_t"][:, :, p * 512:(p + 1) * 512],
                                wait=ev_wu[pp], sem="gw0" if pp == 0 else "gw1")
                    if p == 0 and "wzp0" in b.debug:
                        dump("wzp0", wzpan[0].rearrange("p a n -> p (a n)"),
                             wait=[evw])
                    for t0 in range(NT):
                        bnk = t0 % 2
                        evm = None
                        for k in range(16):
                            evm = b.t(lambda t, k=k, t0=t0, pp=pp, bnk=bnk:
                                      t.matmul(pb[bnk][:, 0:512],
                                               xnT[:, k, t0 * 128:(t0 + 1) * 128],
                                               wzpan[pp][:, k, :],
                                               start=(k == 0), stop=(k == 15)),
                                      wait=[evw, ev_zt[bnk]] if k == 0 else None,
                                      inc="pt" if k == 15 else None)
                        zpp = t0 % 2
                        if p == 0 and t0 == 0 and "zps0" in b.debug:
                            evdbg = b.v(lambda v: v.tensor_copy(
                                wfl0dbg, pb[0][:, 0:256]), wait=evm, inc="pv")
                            dump("zps0", wfl0dbg, wait=[evdbg])
                            dump("xnT_z", xnT.rearrange("p a n -> p (a n)"),
                                 wait=[evdbg])
                        evsl = b.s(lambda s, bnk=bnk, zpp=zpp: s.activation(
                            ztmp[zpp], pb[bnk][:, 0:512], AF.Silu),
                            wait=evm, inc="ps")
                        evzw = b.dma("sync", d_silz[t0, :, p * 512:(p + 1) * 512],
                                     ztmp[zpp], wait=evsl,
                                     sem="dzw0" if t0 % 2 == 0 else "dzw1")
                        ev_zt[bnk] = [evsl, evzw]
                    ev_wu[pp] = b.ev("ps")
                BF.free("ztA", "ztB", "wpA", "wpB")
                b.barrier("zstream")
                ckpt("z")

            b.tag = f"dt{ph}"
            wdtp = BF.reg("wdtp", 1024).rearrange("p (k n) -> p k n", k=16)
            evw = b.dma("gpsimd", wdtp, dx["wdt_t"], sem="gw0")
            dtraw = FL.reg("dtraw", NT * NH).rearrange("p (a n) -> p a n", a=NT)
            ev_pb = None
            for t0 in range(NT):
                evm = None
                for k in range(16):
                    evm = b.t(lambda t, k=k, t0=t0: t.matmul(
                        pb[5][:, 0:NH], xnT[:, k, t0 * 128:(t0 + 1) * 128],
                        wdtp[:, k, :], start=(k == 0), stop=(k == 15)),
                        wait=[evw, ev_pb] if k == 0 else None,
                        inc="pt" if k == 15 else None)
                ev_pb = b.v(lambda v, t0=t0: v.tensor_copy(
                    dtraw[:, t0, :], pb[5][:, 0:NH]), wait=evm, inc="pv")
            dtv = r_dtok.rearrange("p (a n) -> p a n", a=NT)
            dtb_bc = r_dtbbc.unsqueeze(1).broadcast_to([128, NT, NH])
            eva = b.v(lambda v: v.tensor_tensor(
                dtv, dtraw, dtb_bc, ALU.add), wait=b.ev("pv"), inc="pv")
            eve = b.s(lambda s: s.activation(r_dtok, r_dtok, AF.Exp),
                      wait=eva, inc="ps")
            evl = b.s(lambda s: s.activation(r_dtok, r_dtok, AF.Ln, bias=r_cone),
                      wait=eve, inc="ps")
            FL.free("dtraw")
            BF.free("wdtp")
            dump(f"dtok_{ph}", r_dtok, wait=[evl])
            b.barrier(f"dt{ph}")
            ckpt(f"dt{ph}")

            b.tag = f"xbc{ph}"
            bigr = BF.reg("bigbuf", NPC * TH)
            big = bigr.rearrange("p (c n) -> p c n", c=NPC)
            wpan = [BF.reg("wpA", 4096).rearrange("p (k n) -> p k n", k=16),
                    BF.reg("wpB", 4096).rearrange("p (k n) -> p k n", k=16)]
            raw = BF.reg("raw", 2 * (TH + 3)).rearrange(
                "p (c n) -> p c n", c=2)
            cacc = FL.reg("cacA", 2 * TH).rearrange("p (c n) -> p c n", c=2)
            ctmp = FL.reg("cacB", 2 * TH).rearrange("p (c n) -> p c n", c=2)
            if ph == "H":
                haloraw = BF.reg("haloraw", NPC * 3).rearrange(
                    "p (c n) -> p c n", c=NPC)
            else:
                haloraw = BF.at("haloraw").rearrange("p (c n) -> p c n", c=NPC)
            cwv = r_convw.rearrange("p (c k) -> p c k", k=KC)
            panels = [(i * 2, 2) for i in range(17)]
            ev_wu = [None, None]
            ev_raw = None
            ev_silu = None
            for ip, (pc0, npc) in enumerate(panels):
                pp = ip % 2
                wsem = "dw0" if pp == 0 else "dw1"
                evw = b.dma("gpsimd", wpan[pp][:, :, 0:npc * 128],
                            dx["wbc_t"][:, :, pc0 * 128:(pc0 + npc) * 128],
                            wait=ev_wu[pp], sem="gw0" if pp == 0 else "gw1")
                ev_ev = []
                for n in range(2):
                    for jb in range(npc):
                        evm = None
                        for k in range(16):
                            evm = b.t(lambda t, k=k, jb=jb, n=n, pp=pp: t.matmul(
                                pb[jb],
                                wpan[pp][:, k, jb * 128:(jb + 1) * 128],
                                xnT[:, k, n * 512:(n + 1) * 512],
                                start=(k == 0), stop=(k == 15)),
                                wait=([evw, ev_raw] + ev_ev[-npc:]
                                      if k == 0 else None),
                                inc="pt" if k == 15 else None)
                        ev_ev.append(b.v(lambda v, jb=jb, n=n: v.tensor_copy(
                            raw[:, jb, 3 + n * 512:3 + (n + 1) * 512],
                            pb[jb]), wait=evm, inc="pv"))
                if ph == "H":
                    evp = b.v(lambda v, npc=npc: v.memset(raw[:, 0:npc, 0:3], 0.0),
                              inc="pv")
                else:
                    evp = b.v(lambda v, pc0=pc0, npc=npc: v.tensor_copy(
                        raw[:, 0:npc, 0:3], haloraw[:, pc0:pc0 + npc, :]),
                        inc="pv")
                evc = b.v(lambda v, npc=npc, pc0=pc0: v.tensor_tensor(
                    cacc[:, 0:npc, :], raw[:, 0:npc, 0:TH],
                    cwv[:, pc0:pc0 + npc, 0:1].broadcast_to([128, npc, TH]),
                    ALU.mult), wait=ev_ev + [evp, ev_silu], inc="pv")
                for k in range(1, KC):
                    b.v(lambda v, k=k, npc=npc, pc0=pc0: v.tensor_tensor(
                        ctmp[:, 0:npc, :], raw[:, 0:npc, k:k + TH],
                        cwv[:, pc0:pc0 + npc, k:k + 1].broadcast_to([128, npc, TH]),
                        ALU.mult), inc="pv")
                    evc = b.v(lambda v, npc=npc: v.tensor_tensor(
                        cacc[:, 0:npc, :], cacc[:, 0:npc, :], ctmp[:, 0:npc, :],
                        ALU.add), inc="pv")
                if ph == "H":
                    b.v(lambda v, pc0=pc0, npc=npc: v.tensor_copy(
                        haloraw[:, pc0:pc0 + npc, :], raw[:, 0:npc, TH:TH + 3]),
                        inc="pv")
                ev_raw = b.ev("pv")
                evs = b.s(lambda s, pc0=pc0, npc=npc: s.activation(
                    big[:, pc0:pc0 + npc, :], cacc[:, 0:npc, :], AF.Silu),
                    wait=ev_raw, inc="ps")
                ev_silu = evs
                ev_wu[pp] = evs
                wabs_pump(6)
            BF.free("raw")
            FL.free("cacA", "cacB")
            BF.free("wpA", "wpB")
            BF.free("xnT")
            b.barrier(f"conv{ph}")
            ckpt(f"conv{ph}")
            dump(f"conv_{ph}", bigr, wait=[b.ev("ps")])

            b.tag = f"chunks{ph}"
            xs_r = BF.reg("xs", 32 * 128)
            xs = xs_r.rearrange("p (c n) -> p c n", c=32)
            btok = BF.reg("btok", 128)
            bw8 = BF.reg("bw", 8 * 128).rearrange("p (c n) -> p c n", c=8)
            aU = BF.reg("aU", 8 * 128).rearrange("p (c n) -> p c n", c=8)
            Lg = BF.reg("Lg", 1024)
            Lgv = Lg.rearrange("p (a n) -> p a n", a=8)
            Mt = BF.reg("Mt", 8 * 128).rearrange("p (c n) -> p c n", c=8)
            ev1t = BF.reg("ev1t", 512)
            ev2t = BF.reg("ev2t", 512)
            ych = BF.reg("ych", HID)
            sq2 = BF.reg("sq2", HID)
            szb = [BF.reg("szA", HID), BF.reg("szB", HID)] if is_m else None

            ckall = FL.reg("ckall", 4 * 512)
            a_v = ckall[:, 0:512].rearrange("p (c n) -> p c n", c=8)
            cum_all = ckall[:, 512:1024]
            wdec_all = ckall[:, 1024:1536]
            wdec_v = wdec_all.rearrange("p (c n) -> p c n", c=8)
            dec_v = ckall[:, 1536:2048].rearrange("p (c n) -> p c n", c=8)
            ebf_r = BF.reg("ebfall", 512) if is_m else None
            ebf_v = (ebf_r.rearrange("p (c n) -> p c n", c=8)
                     if is_m else None)
            eva = b.v(lambda v: v.tensor_tensor(
                a_v, dtv, r_abc.unsqueeze(1).broadcast_to([128, NT, NH]),
                ALU.mult), inc="pv")
            evcm = b.t(lambda t: t.matmul(pb[5][:, 0:512], r_uincl,
                                          ckall[:, 0:512],
                                          start=True, stop=True),
                       wait=eva, inc="pt")
            evcum = b.v(lambda v: v.tensor_copy(cum_all, pb[5][:, 0:512]),
                        wait=evcm, inc="pv")
            evwm = b.t(lambda t: t.matmul(pb[5][:, 0:512], r_lstrict,
                                          ckall[:, 0:512],
                                          start=True, stop=True),
                       wait=evcum, inc="pt")
            evwe = b.s(lambda s: s.activation(wdec_all, pb[5][:, 0:512],
                                              AF.Exp), wait=evwm, inc="ps")
            evwd = b.v(lambda v: v.tensor_tensor(
                wdec_all, wdec_all, r_dtok, ALU.mult), wait=evwe, inc="pv")
            evdm = b.t(lambda t: t.matmul(pb[5][0:1, 0:512], r_onescol,
                                          ckall[:, 0:512],
                                          start=True, stop=True),
                       wait=evwe, inc="pt")
            evde = b.s(lambda s: s.activation(r_drow[0:1, :],
                                              pb[5][0:1, 0:512],
                                              AF.Exp), wait=evdm, inc="ps")
            evdb = b.t(lambda t: t.matmul(pb[5][:, 0:512],
                                          r_ones1x[0:1, :], r_drow[0:1, :],
                                          start=True, stop=True),
                       wait=evde, inc="pt")
            evdbc = b.v(lambda v: v.tensor_copy(ckall[:, 1536:2048],
                                                pb[5][:, 0:512]),
                        wait=evdb, inc="pv")
            eve_all = (b.s(lambda s: s.activation(
                ebf_r, cum_all, AF.Exp), wait=evcum, inc="ps")
                if is_m else None)

            dump(f"ckall_{ph}", ckall, wait=[evdbc, evwd, b.ev("ps")])
            ev_sz = [None, None]
            ev_szu = [None, None]
            ev_scan_prev = None
            ev_gate_prev = None
            ev_bwmm = None
            ev_auE = None
            ev_ynT_prev = None
            ev_gtprev = evdbc
            ev_sread = [evdbc, evdbc]
            nch_eff = 1 if (is_m and "c1" in skips) else NCH
            for c in range(nch_eff):
                cw = slice(c * 128, (c + 1) * 128)
                a_c = a_v[:, c, :]
                if is_m and "sz" not in skips:
                    pp = c % 2
                    ev_sz[pp] = b.dma("sync", szb[pp], d_silz[c, :, :],
                                      wait=[ev_szu[pp], b.ev("dzw0"),
                                            b.ev("dzw1")], sem=f"dsz{pp}")
                evT = []
                for m in range(32):
                    evT.append(pe_T(xs[:, m, :], big[:, m, cw], bf=True,
                                    bank=m % 2, eng="s", wait=ev_gate_prev))
                evB = pe_T(btok, big[:, 32, cw], bf=True, bank=0, eng="s",
                           wait=ev_bwmm)
                if is_m and "egt" not in skips:
                    evg = b.t(lambda t, cw=cw: t.matmul(
                        pb[5][:, 256:384], big[:, 32, cw], big[:, 33, cw],
                        start=True, stop=True), wait=[ev_gtprev],
                        inc="pt")
                    evgm = b.v(lambda v: v.tensor_tensor(
                        r_gt, pb[5][:, 256:384], r_uincl, ALU.mult),
                        wait=evg, inc="pv")
                    ev_gtprev = evgm
                ev_gprev = None
                for g in range(8):
                    hs = slice(g * 8 * PH, (g + 1) * 8 * PH)
                    if is_m and "inner" not in skips:
                        evau = [b.v(lambda v, g=g, a_c=a_c: v.tensor_tensor(
                            aU,
                            r_uincl_bf.unsqueeze(1).broadcast_to([128, 8, 128]),
                            a_c[:, g * 8:(g + 1) * 8].unsqueeze(2)
                            .broadcast_to([128, 8, 128]),
                            ALU.mult), wait=[eva, ev_auE], inc="pv")]
                        evE = None
                        for h8 in range(8):
                            evE = b.t(lambda t, h8=h8: t.matmul(
                                pb[3 + h8 // 4][:, (h8 % 4) * 128:(h8 % 4 + 1) * 128],
                                r_ustrict_bf, aU[:, h8, :],
                                start=(h8 % 4 == 0), stop=(h8 % 4 == 3)),
                                wait=(evau + [ev_gprev] if h8 == 0 else None),
                                inc="pt" if h8 == 7 else None)
                        ev_auE = evE
                        evL1 = b.s(lambda s: s.activation(Lg[:, 0:512], pb[3],
                                                          AF.Exp), wait=evE, inc="ps")
                        evL2 = b.s(lambda s: s.activation(Lg[:, 512:1024], pb[4],
                                                          AF.Exp), wait=evE, inc="ps")
                        evMt1 = b.v(lambda v, g=g, c=c: v.tensor_tensor(
                            Mt, Lgv,
                            dtv[:, c, g * 8:(g + 1) * 8].unsqueeze(2)
                            .broadcast_to([128, 8, 128]),
                            ALU.mult), wait=[evL1, evL2], inc="pv")
                        evMt = [b.v(lambda v: v.tensor_tensor(
                            Mt, Mt,
                            r_gt.unsqueeze(1).broadcast_to([128, 8, 128]),
                            ALU.mult), wait=[evMt1, evgm], inc="pv")]
                        if c == 0 and g == 0 and "mt0" in b.debug:
                            dump("mt0", Mt.rearrange("p a n -> p (a n)"),
                                 wait=evMt)
                            dump("lg0", Lg, wait=evMt)
                            dump("gt0", r_gt, wait=evMt)
                            dump("au0", aU.rearrange("p a n -> p (a n)"),
                                 wait=evMt)
                        evYo = None
                        for h8 in range(8):
                            h = g * 8 + h8
                            evYo = b.t(lambda t, h=h, h8=h8, cw=cw: t.matmul(
                                pb[1][:, h8 * PH:(h8 + 1) * PH], big[:, 33, cw],
                                r_S[:, h * PH:(h + 1) * PH],
                                start=(h8 == 0), stop=(h8 == 7)),
                                wait=([ev_scan_prev, evgm, ev_gprev, b.ev("pv")]
                                      if h8 == 0 else None),
                                inc="pt" if h8 == 7 else None)
                        evYd = None
                        for h8 in range(8):
                            h = g * 8 + h8
                            xsl = xs_r[:, h * PH:(h + 1) * PH]
                            evYd = b.t(lambda t, h8=h8, xsl=xsl: t.matmul(
                                pb[0][:, h8 * PH:(h8 + 1) * PH], Mt[:, h8, :], xsl,
                                start=(h8 == 0), stop=(h8 == 7)),
                                wait=(evMt + evT + [evB, ev_gprev]
                                      if h8 == 0 else None),
                                inc="pt" if h8 == 7 else None)
                    evbw = b.v(lambda v, g=g, c=c: v.tensor_tensor(
                        bw8,
                        btok.unsqueeze(1).broadcast_to([128, 8, 128]),
                        wdec_v[:, c, g * 8:(g + 1) * 8].unsqueeze(2)
                        .broadcast_to([128, 8, 128]),
                        ALU.mult),
                        wait=([evwd, evB, ev_bwmm] + (evT if g == 0 else [])),
                        inc="pv")
                    sb = 2 if is_m else 2 + (g % 2)
                    evst = None
                    for h8 in range(8):
                        h = g * 8 + h8
                        xsl = xs_r[:, h * PH:(h + 1) * PH]
                        evst = b.t(lambda t, h8=h8, xsl=xsl, sb=sb: t.matmul(
                            pb[sb][:, h8 * PH:(h8 + 1) * PH], bw8[:, h8, :], xsl,
                            start=(h8 == 0), stop=(h8 == 7)),
                            wait=[evbw] + ([ev_sread[sb - 2]] if h8 == 0 else []),
                            inc="pt")
                    ev_bwmm = evst
                    if is_m and "inner" not in skips:
                        evv1 = b.v(lambda v, g=g, c=c: v.tensor_tensor(
                            ev1t, pb[1].rearrange("p (a n) -> p a n", a=8),
                            ebf_v[:, c, g * 8:(g + 1) * 8].unsqueeze(2)
                            .broadcast_to([128, 8, PH]),
                            ALU.mult), wait=[evYo, eve_all], inc="pv")
                        evv2 = b.v(lambda v, hs=hs: v.tensor_tensor(
                            ych[:, hs], ev1t, pb[0], ALU.add),
                            wait=[evv1, evYd, ev_ynT_prev], inc="pv")
                        evv3 = b.v(lambda v, g=g, hs=hs: v.tensor_tensor(
                            ev1t.rearrange("p (a e n) -> p a e n", a=4, e=2),
                            xs_r[:, hs].rearrange("p (a n) -> p a n", a=4),
                            r_dbc[:, g * 8:(g + 1) * 8]
                            .rearrange("p (a e) -> p a e", a=4).unsqueeze(3)
                            .broadcast_to([128, 4, 2, PH]),
                            ALU.mult), wait=evv2, inc="pv")
                        evv4 = b.v(lambda v, hs=hs: v.tensor_tensor(
                            ych[:, hs], ych[:, hs], ev1t, ALU.add),
                            wait=evv3, inc="pv")
                    if c == 0 and g == 0 and f"stbank_{ph}" in b.debug:
                        evdbg = b.v(lambda v, sb=sb: v.tensor_copy(
                            sq2[:, 0:512], pb[sb]), wait=[evst], inc="pv")
                        evdbg2 = b.v(lambda v: v.tensor_copy(
                            sq2[:, 512:640], bw8[:, 0, :]),
                            wait=[evdbg], inc="pv")
                        dump(f"stbank_{ph}", sq2[:, 0:640], wait=[evdbg2])
                    evs1 = b.v(lambda v, g=g, c=c, hs=hs: v.tensor_tensor(
                        ev2t, r_S[:, hs].rearrange("p (a n) -> p a n", a=8),
                        dec_v[:, c, g * 8:(g + 1) * 8].unsqueeze(2)
                        .broadcast_to([128, 8, PH]), ALU.mult),
                        wait=[evdbc, evst] + ([evYo] if (is_m and "inner" not in skips) else []), inc="pv")
                    _oldtag = b.tag
                    b.tag = f"SCAN_{ph}_{c}_{g}"
                    evs2 = b.v(lambda v, hs=hs, sb=sb: v.tensor_tensor(
                        r_S[:, hs], ev2t, pb[sb], ALU.add), wait=evs1, inc="pv")
                    b.tag = _oldtag
                    ev_sread[sb - 2] = evs2
                    if c == 0 and g == 0 and f"Sg0_{ph}" in b.debug:
                        dump(f"Sg0_{ph}", r_S[:, 0:512], wait=[evs2])
                    if c == 0 and g == 0 and f"ev2t_{ph}" in b.debug:
                        dump(f"ev2t_{ph}", ev2t, wait=[evs1])
                    ev_gprev = b.ev("pv")
                if is_m and c == 0 and "ych0" in b.debug:
                    dump("ych0", ych, wait=[b.ev("pv")])
                ev_scan_prev = b.ev("pv")
                if c == 0:
                    dump(f"S_c0_{ph}", r_S, wait=[ev_scan_prev])
                    dump(f"xs_c0_{ph}", xs_r, wait=[ev_scan_prev])
                    dump(f"btok_c0_{ph}", btok, wait=[ev_scan_prev])
                if is_m and "gate" not in skips:
                    pp = c % 2
                    if c == 0 and "sz0" in b.debug:
                        dump("sz0", szb[pp], wait=[ev_sz[pp]])
                    evyz = b.v(lambda v, pp=pp: v.tensor_tensor(
                        ych, ych, szb[pp], ALU.mult),
                        wait=[ev_sz[pp], ev_scan_prev], inc="pv")
                    ev_szu[pp] = evyz
                    evss = b.v(lambda v: v.scalar_tensor_tensor(
                        sq2, ych, 1.0, ych, ALU.mult, ALU.mult,
                        accum_out=r_rn[:, 0:1]), wait=evyz, inc="pv")
                    evsr = b.s(lambda s: s.activation(
                        r_rn[:, 1:2], r_rn[:, 0:1], AF.Sqrt, bias=r_cepsr,
                        scale=r_cinvh), wait=evss, inc="ps")
                    evrc = b.v(lambda v: v.reciprocal(r_rn[:, 2:3], r_rn[:, 1:2]),
                               wait=evsr, inc="pv")
                    evyn = b.v(lambda v: v.tensor_scalar(
                        ych, ych, r_rn[:, 2:3], None, ALU.mult),
                        wait=evrc, inc="pv")
                    if c == 0 and "yn0" in b.debug:
                        dump("yn0", ych, wait=[evyn])
                    for k in range(32):
                        ev_ynT_prev = pe_T(
                            big[:, k, cw], ych[:, k * 128:(k + 1) * 128],
                            bf=True, bank=k % 2, wait=evyn,
                            scale_col=r_normw[:, k:k + 1], eng="s")
                ev_gate_prev = b.ev("pv")
                wabs_pump(2)
            BF.free("xs", "btok", "bw", "aU", "Lg", "Mt",
                    "ev1t", "ev2t", "ych", "sq2")
            FL.free("ckall")
            if is_m:
                BF.free("szA", "szB", "ebfall")
            b.barrier(f"chunks{ph}")
            ckpt(f"chunks{ph}")
            if ph == "H":
                dump("S_mid", r_S, wait=[b.ev("pv")])
                BF.free("bigbuf")
            return

        run_phase("H")
        ckpt("H")
        wabs_pump(128)
        wabs_reduce()
        b.barrier("wabs")
        ckpt("wabs")
        run_phase("M")
        ckpt("M")
        dump("ynT", BF.at("bigbuf"), wait=[b.ev("pv")])

        big = BF.at("bigbuf").rearrange("p (c n) -> p c n", c=NPC)
        b.tag = "wout"
        wpo = [BF.reg("woA", 8192).rearrange("p (k n) -> p k n", k=32),
               BF.reg("woB", 8192).rearrange("p (k n) -> p k n", k=32)]
        atmp = [BF.reg("atA", 512), BF.reg("atB", 512)]
        ev_wu = [None, None]
        ev_at = [None, None]
        ev_bank = [None] * 4
        for p in range(8):
            pp = p % 2
            evw = b.dma("gpsimd", wpo[pp], dx["wout_t"][:, :, p * 256:(p + 1) * 256],
                        wait=ev_wu[pp], sem="gw0" if pp == 0 else "gw1")
            for jb in range(2):
                jbg = p * 2 + jb
                for n in range(2):
                    bnk = jb * 2 + n
                    evm = None
                    for k in range(32):
                        evm = b.t(lambda t, k=k, jb=jb, n=n, pp=pp, bnk=bnk:
                                  t.matmul(pb[bnk],
                                           wpo[pp][:, k, jb * 128:(jb + 1) * 128],
                                           big[:, k, n * 512:(n + 1) * 512],
                                           start=(k == 0), stop=(k == 31)),
                                  wait=[evw, ev_bank[bnk]] if k == 0 else None,
                                  inc="pt" if k == 31 else None)
                    app = (jbg * 2 + n) % 2
                    eva = b.v(lambda v, bnk=bnk, app=app: v.tensor_copy(
                        atmp[app], pb[bnk]), wait=[evm, ev_at[app]], inc="pv")
                    evd = b.dma("sync", d_sum[jbg, :, n * 512:(n + 1) * 512],
                                atmp[app], wait=eva, sem=f"daw{app}")
                    ev_at[app] = evd
                    ev_bank[bnk] = eva
            ev_wu[pp] = b.ev("pv")
        BF.free("woA", "woB", "atA", "atB", "bigbuf")
        b.barrier("wout")
        ckpt("wout")

        b.tag = "qcols"
        cq = FL.reg("colq", 8)
        evd1 = b.v(lambda v: v.tensor_scalar(cq[:, 0:8], r_var, float(DIM), None,
                                             ALU.mult), inc="pv")
        evd2 = b.s(lambda s: s.activation(r_beta, r_var, AF.Identity, bias=r_ceps,
                                          scale=1.0), wait=evd1, inc="ps")
        evd3 = b.v(lambda v: v.reciprocal(r_beta, r_beta), wait=evd2, inc="pv")
        evd4 = b.v(lambda v: v.tensor_tensor(cq[:, 0:8], cq[:, 0:8], r_beta,
                                             ALU.mult), wait=evd3, inc="pv")
        evd5 = b.s(lambda s: s.activation(cq[:, 0:8], cq[:, 0:8], AF.Sqrt),
                   wait=evd4, inc="ps")
        evd6 = b.v(lambda v: v.tensor_scalar(cq[:, 0:8], cq[:, 0:8], 1e-12, None,
                                             ALU.max), wait=evd5, inc="pv")
        evd7 = b.v(lambda v: v.reciprocal(r_beta, cq[:, 0:8]), wait=evd6, inc="pv")
        evd8 = b.v(lambda v: v.tensor_scalar(r_beta, r_beta, math.sqrt(DIM), None,
                                             ALU.mult), wait=evd7, inc="pv")
        evd9 = b.v(lambda v: v.tensor_tensor(cq[:, 0:8], r_beta, r_amax, ALU.mult),
                   wait=evd8, inc="pv")
        evd10 = b.v(lambda v: v.tensor_scalar(cq[:, 0:8], cq[:, 0:8], 1e-5, None,
                                              ALU.max), wait=evd9, inc="pv")
        evd11 = b.v(lambda v: v.reciprocal(r_dq, cq[:, 0:8]), wait=evd10, inc="pv")
        evd12 = b.v(lambda v: v.tensor_tensor(r_beta, r_beta, r_dq, ALU.mult),
                    wait=evd11, inc="pv")
        evd13 = b.v(lambda v: v.tensor_scalar(r_beta, r_beta, 127.0, None,
                                              ALU.mult), wait=evd12, inc="pv")
        evd14 = b.v(lambda v: v.tensor_scalar(r_dq, cq[:, 0:8], 1.0 / 127.0, None,
                                              ALU.mult), wait=evd13, inc="pv")
        evd15 = b.v(lambda v: v.tensor_scalar(r_dq, r_dq, r_m12[:, 0:1], None,
                                              ALU.mult), wait=evd14, inc="pv")
        evd16 = b.v(lambda v: v.tensor_scalar(r_nd384, r_dq, -1536.0, None,
                                              ALU.mult), wait=evd15, inc="pv")
        FL.free("colq")
        b.barrier("qcols")
        ckpt("qcols")

        b.tag = "ffn"
        hh_r = BF.reg("hh", 64 * 512)
        hh = hh_r.rearrange("p (c n) -> p c n", c=64)
        xqT = BF.reg("xqT", 16 * 512).rearrange("p (k n) -> p k n", k=16)
        wq = [BF.reg("wqA", 4096).rearrange("p (k n) -> p k n", k=16),
              BF.reg("wqB", 4096).rearrange("p (k n) -> p k n", k=16)]
        pos = BF.reg("pos", 4096)
        neg = BF.reg("neg", 4096)
        usc = BF.reg("usc", DIM)
        u2 = FL.reg("u2", 512)
        sqh = [FL.reg("sqhA", 512), FL.reg("sqhB", 512)]
        stmp = [BF.reg("stA", 512), BF.reg("stB", 512)]
        sc1 = FL.reg("sc1", 512)
        hrow = FL.reg("hrow", 512)
        hax = FL.reg("hax", 16)
        bbc = FL.reg("bbc", 512)
        dbc2 = FL.reg("dbc2", 512)

        for hf in range(2):
            tks = range(hf * 4, hf * 4 + 4)
            xt2 = FL.reg("xtile", 2 * DIM).rearrange("p (b n) -> p b n", b=2)
            ev_xt = [None, None]
            ev_usc = None
            for i, t0 in enumerate(tks):
                pp = i % 2
                xt = xt2[:, pp, :]
                evl = b.dma("sync", xt, dx["x_own"][t0 * 128:(t0 + 1) * 128, :],
                            wait=ev_xt[pp], sem="dx0" if pp == 0 else "dx1")
                evxn = b.s(lambda s, xt=xt, t0=t0: s.activation(
                    xt, xt, AF.Identity, bias=r_nmrs[:, t0:t0 + 1],
                    scale=r_rscol[:, t0:t0 + 1]), wait=[evl, ev_usc], inc="ps")
                evu = b.v(lambda v, xt=xt, t0=t0: v.tensor_scalar(
                    r_u16x, xt, r_beta[:, t0:t0 + 1], 1536.0, ALU.mult, ALU.add),
                    wait=evxn, inc="pv")
                evq = b.v(lambda v, t0=t0: v.tensor_scalar(
                    usc, r_u16x, r_dq[:, t0:t0 + 1], r_nd384[:, t0:t0 + 1],
                    ALU.mult, ALU.add), wait=evu, inc="pv")
                for k in range(16):
                    pe_T(xqT[:, k, i * 128:(i + 1) * 128],
                         usc[:, k * 128:(k + 1) * 128], bf=True, bank=k % 2,
                         wait=evq)
                ev_usc = b.ev("pv")
                ev_xt[pp] = b.ev("pv")
            b.barrier(f"xq{hf}")
            if hf == 0 and "xqT0" in b.debug:
                dump("xqT0", xqT.rearrange("p a n -> p (a n)"), wait=[b.ev("pv")])
            FL.free("xtile")
            wfl = FL.reg("wf", 2 * 4096).rearrange("p (b n) -> p b n", b=2)
            ev_wu = [None, None]
            ev_bank = [None, None]
            ev_qprev = None
            for p in range(32):
                pp = p % 2
                evw = b.dma("sync", wfl[:, pp, :],
                            dx["w1_t"][:, :, p * 256:(p + 1) * 256],
                            wait=ev_wu[pp], sem="dw0" if pp == 0 else "dw1")
                evp_ = b.g(lambda g, pp=pp, wfl=wfl: g.tensor_scalar(
                    pos, wfl[:, pp, :], r_thr1[:, 0:1], None, ALU.is_ge),
                    wait=[evw, ev_qprev], inc="pg")
                evn_ = b.g(lambda g, pp=pp, wfl=wfl: g.tensor_scalar(
                    neg, wfl[:, pp, :], r_thr1[:, 1:2], None, ALU.is_le),
                    wait=evw, inc="pg")
                evq_ = b.v(lambda v, pp=pp: v.tensor_tensor(
                    wq[pp].rearrange("p a n -> p (a n)"), pos, neg, ALU.subtract),
                    wait=[evp_, evn_], inc="pv")
                ev_qprev = evq_
                ev_wu[pp] = evq_
                for jb in range(2):
                    jbg = p * 2 + jb
                    bnk = jb
                    evm = None
                    for k in range(16):
                        evm = b.t(lambda t, k=k, jb=jb, pp=pp, bnk=bnk: t.matmul(
                            pb[bnk], wq[pp][:, k, jb * 128:(jb + 1) * 128],
                            xqT[:, k, :], start=(k == 0), stop=(k == 15)),
                            wait=[evq_, ev_bank[bnk]] if k == 0 else None,
                            inc="pt" if k == 15 else None)
                    ev_bank[bnk] = b.s(lambda s, jbg=jbg, bnk=bnk: s.activation(
                        hh[:, jbg, :], pb[bnk], AF.Gelu), wait=evm, inc="ps")
                    spp = jbg % 2
                    evsq = b.v(lambda v, jbg=jbg, spp=spp: v.tensor_tensor(
                        sqh[spp], hh[:, jbg, :], hh[:, jbg, :], ALU.mult),
                        wait=[ev_bank[bnk]], inc="pv")
                    if jbg == 0:
                        b.v(lambda v: v.tensor_copy(u2, sqh[0]),
                            wait=evsq, inc="pv")
                        b.v(lambda v: v.tensor_copy(sc1, sqh[0]), inc="pv")
                    else:
                        b.v(lambda v, spp=spp: v.tensor_tensor(
                            u2, u2, sqh[spp], ALU.max), wait=evsq, inc="pv")
                        b.v(lambda v, spp=spp: v.tensor_tensor(
                            sc1, sc1, sqh[spp], ALU.add), inc="pv")
                ev_wu[pp] = [evq_, ev_bank[0], ev_bank[1]]
            b.barrier(f"w1_{hf}")
            if hf == 0 and "hh0" in b.debug:
                dump("hh0", hh_r, wait=[b.ev("ps")])
            evn2m = b.t(lambda t: t.matmul(pb[4][0:1, 0:512], r_onescol, sc1,
                                           start=True, stop=True), inc="pt")
            evh2 = b.v(lambda v: v.tensor_copy(hrow[0:1, 0:512], pb[4][0:1, 0:512]),
                       wait=evn2m, inc="pv")
            ev_tb = [None, None]
            for i in range(4):
                pbk = pb[4] if i % 2 == 0 else pb[5]
                evt_ = b.t(lambda t, i=i, pbk=pbk: t.transpose(
                    pbk[:, 0:128], u2[:, i * 128:(i + 1) * 128], r_eye_f),
                    wait=[evh2, ev_tb[i % 2]], inc="pt")
                ev_tb[i % 2] = b.v(lambda v, i=i, pbk=pbk: v.reduce_max(
                    hax[:, i:i + 1], pbk[:, 0:128], AX.X,
                    apply_absolute_value=True), wait=evt_, inc="pv")
            evsx = b.s(lambda s: s.activation(hax[:, 0:4], hax[:, 0:4], AF.Sqrt),
                       wait=b.ev("pv"), inc="ps")
            for i in range(4):
                evt_ = b.t(lambda t, i=i: t.transpose(
                    pb[4][:, 0:1], hrow[0:1, i * 128:(i + 1) * 128],
                    r_ones1x[0:1, 0:1]), wait=[evh2, b.ev("pv"), evsx], inc="pt")
                b.v(lambda v, i=i: v.tensor_copy(hax[:, 4 + i:5 + i],
                                                 pb[4][:, 0:1]),
                    wait=evt_, inc="pv")
            evc1 = b.s(lambda s: s.activation(hax[:, 8:12], hax[:, 4:8], AF.Sqrt),
                       wait=b.ev("pv"), inc="ps")
            evc2 = b.v(lambda v: v.tensor_scalar(hax[:, 8:12], hax[:, 8:12],
                                                 1e-12, None, ALU.max),
                       wait=evc1, inc="pv")
            evc3 = b.v(lambda v: v.reciprocal(hax[:, 8:12], hax[:, 8:12]),
                       wait=evc2, inc="pv")
            evc4 = b.v(lambda v: v.tensor_scalar(hax[:, 8:12], hax[:, 8:12],
                                                 math.sqrt(8192.0), None, ALU.mult),
                       wait=evc3, inc="pv")
            evc5 = b.v(lambda v: v.tensor_tensor(hax[:, 12:16], hax[:, 8:12],
                                                 hax[:, 0:4], ALU.mult),
                       wait=[evc4, evsx], inc="pv")
            evc6 = b.v(lambda v: v.tensor_scalar(hax[:, 12:16], hax[:, 12:16],
                                                 1e-5, None, ALU.max),
                       wait=evc5, inc="pv")
            evc7 = b.v(lambda v: v.reciprocal(hax[:, 0:4], hax[:, 12:16]),
                       wait=evc6, inc="pv")
            evc8 = b.v(lambda v: v.tensor_tensor(hax[:, 8:12], hax[:, 8:12],
                                                 hax[:, 0:4], ALU.mult),
                       wait=evc7, inc="pv")
            evc9 = b.v(lambda v: v.tensor_scalar(hax[:, 8:12], hax[:, 8:12],
                                                 127.0, None, ALU.mult),
                       wait=evc8, inc="pv")
            evc10 = b.v(lambda v: v.tensor_scalar(hax[:, 12:16], hax[:, 12:16],
                                                  1.0 / 127.0, None, ALU.mult),
                        wait=evc9, inc="pv")
            evc11 = b.v(lambda v: v.tensor_scalar(hax[:, 12:16], hax[:, 12:16],
                                                  r_m12[:, 1:2], None, ALU.mult),
                        wait=evc10, inc="pv")
            for i in range(4):
                evt_ = b.t(lambda t, i=i: t.transpose(
                    pb[4][0:1, 0:128], hax[:, 8 + i:9 + i], r_eye_f),
                    wait=[evc11, b.ev("pv")], inc="pt")
                evr_ = b.v(lambda v: v.tensor_copy(hrow[0:1, 0:128],
                                                   pb[4][0:1, 0:128]),
                           wait=evt_, inc="pv")
                evb_ = b.t(lambda t: t.matmul(pb[5][:, 0:128], r_ones1x[0:1, :],
                                              hrow[0:1, 0:128],
                                              start=True, stop=True),
                           wait=evr_, inc="pt")
                b.v(lambda v, i=i: v.tensor_copy(bbc[:, i * 128:(i + 1) * 128],
                                                 pb[5][:, 0:128]),
                    wait=evb_, inc="pv")
                evt_ = b.t(lambda t, i=i: t.transpose(
                    pb[4][0:1, 0:128], hax[:, 12 + i:13 + i], r_eye_f),
                    wait=b.ev("pv"), inc="pt")
                evr_ = b.v(lambda v: v.tensor_copy(hrow[0:1, 0:128],
                                                   pb[4][0:1, 0:128]),
                           wait=evt_, inc="pv")
                evb_ = b.t(lambda t: t.matmul(pb[5][:, 0:128], r_ones1x[0:1, :],
                                              hrow[0:1, 0:128],
                                              start=True, stop=True),
                           wait=evr_, inc="pt")
                b.v(lambda v, i=i: v.tensor_copy(dbc2[:, i * 128:(i + 1) * 128],
                                                 pb[5][:, 0:128]),
                    wait=evb_, inc="pv")
            b.barrier(f"hstats{hf}")
            hq_ev = [None] * 4
            for ks in range(4):
                for j in range(ks * 16, ks * 16 + 16):
                    ev1_ = b.v(lambda v, j=j: v.tensor_tensor(
                        sc1, hh[:, j, :], bbc, ALU.mult), inc="pv")
                    ev2_ = b.v(lambda v: v.tensor_scalar(
                        r_u16h, sc1, 1536.0, None, ALU.add), wait=ev1_, inc="pv")
                    b.v(lambda v, j=j: v.scalar_tensor_tensor(
                        hh[:, j, :], r_u16h, -1536.0, dbc2, ALU.add, ALU.mult),
                        wait=ev2_, inc="pv")
                hq_ev[ks] = b.ev("pv")
            if hf == 0 and "hq0" in b.debug:
                dump("hq0", hh_r, wait=[b.ev("pv")])
                dump("hax0", hax, wait=[b.ev("pv")])
                dump("bbc0", bbc, wait=[b.ev("pv")])
                dump("dbc0", dbc2, wait=[b.ev("pv")])
            ev_wu = [None, None]
            ev_bank2 = [None] * 4
            ev_ar = [None, None]
            ev_qprev = None
            for p in range(8):
                for ks in range(4):
                    pp = (p * 4 + ks) % 2
                    evw = b.dma("sync", wfl[:, pp, :],
                                dx["w2_t"][:, ks * 16:(ks + 1) * 16,
                                           p * 256:(p + 1) * 256],
                                wait=ev_wu[pp], sem="dw0" if pp == 0 else "dw1")
                    evp_ = b.g(lambda g, pp=pp, wfl=wfl: g.tensor_scalar(
                        pos, wfl[:, pp, :], r_thr1[:, 2:3], None, ALU.is_ge),
                        wait=[evw, ev_qprev], inc="pg")
                    evn_ = b.g(lambda g, pp=pp, wfl=wfl: g.tensor_scalar(
                        neg, wfl[:, pp, :], r_thr1[:, 3:4], None, ALU.is_le),
                        wait=evw, inc="pg")
                    evq_ = b.v(lambda v, pp=pp: v.tensor_tensor(
                        wq[pp].rearrange("p a n -> p (a n)"), pos, neg,
                        ALU.subtract), wait=[evp_, evn_], inc="pv")
                    ev_qprev = evq_
                    ev_wu[pp] = evq_
                    for jb in range(2):
                        bnk = (p % 2) * 2 + jb
                        evm = None
                        for k in range(16):
                            evm = b.t(lambda t, k=k, jb=jb, pp=pp, ks=ks, bnk=bnk:
                                      t.matmul(pb[bnk],
                                               wq[pp][:, k, jb * 128:(jb + 1) * 128],
                                               hh[:, ks * 16 + k, :],
                                               start=(ks == 0 and k == 0),
                                               stop=(ks == 3 and k == 15)),
                                      wait=([evq_, ev_bank2[bnk], hq_ev[ks]]
                                            if k == 0 else None),
                                      inc="pt" if k == 15 else None)
                        ev_wu[pp] = [evq_, evm]
                        if ks == 3:
                            jbg = p * 2 + jb
                            app = jbg % 2
                            evar = b.dma("sync", stmp[app],
                                         d_sum[jbg, :, hf * 512:(hf + 1) * 512],
                                         wait=[b.ev("daw0"), b.ev("daw1"),
                                               ev_ar[app]],
                                         sem="dar" if app == 0 else "dar1")
                            evsum = b.v(lambda v, bnk=bnk, app=app:
                                        v.scalar_tensor_tensor(
                                            stmp[app], pb[bnk], 1.0, stmp[app],
                                            ALU.mult, ALU.add),
                                        wait=[evm, evar], inc="pv")
                            evwr = b.dma("sync",
                                         d_sum[jbg, :, hf * 512:(hf + 1) * 512],
                                         stmp[app], wait=evsum, sem=f"daw{app}")
                            ev_ar[app] = evsum
                            ev_bank2[bnk] = evsum
            b.barrier(f"w2_{hf}")
            FL.free("wf")
            ckpt(f"ffn{hf}")
        BF.free("hh", "xqT", "wqA", "wqB", "pos", "neg", "usc", "u2",
                "sqhA", "sqhB", "stA", "stB")
        FL.free("sc1", "hrow", "hax", "bbc", "dbc2")

        b.tag = "final"
        fins = [BF.reg("finA", 16 * 128).rearrange("p (k n) -> p k n", k=16),
                BF.reg("finB", 16 * 128).rearrange("p (k n) -> p k n", k=16)]
        xt2 = FL.reg("xtile", 2 * DIM).rearrange("p (b n) -> p b n", b=2)
        ot2 = FL.reg("otile", 2 * DIM).rearrange("p (b n) -> p b n", b=2)
        ev_fin = [None, None]
        ev_ot = [None, None]
        for t0 in range(NT):
            pp = t0 % 2
            evf = b.dma("sync", fins[pp],
                        d_sum[:, :, t0 * 128:(t0 + 1) * 128].transpose([1, 0, 2]),
                        wait=[ev_fin[pp], b.ev("daw0"), b.ev("daw1")],
                        sem="dar" if pp == 0 else "dar1")
            evx = b.dma("sync", xt2[:, pp, :],
                        dx["x_own"][t0 * 128:(t0 + 1) * 128, :],
                        wait=ev_ot[pp], sem="dx0" if pp == 0 else "dx1")
            evs = []
            for k in range(16):
                evs.append(pe_T(ot2[:, pp, k * 128:(k + 1) * 128],
                                fins[pp][:, k, :], bf=True, bank=k % 2,
                                wait=[evf, evx],
                                add_ap=xt2[:, pp, k * 128:(k + 1) * 128]))
            evo = b.dma("sync", dout[t0 * 128:(t0 + 1) * 128, :], ot2[:, pp, :],
                        wait=evs, sem="dout0" if pp == 0 else "dout1")
            ev_fin[pp] = evs
            ev_ot[pp] = evo
        b.sy(lambda e: e.nop(), wait=[b.ev("dout0"), b.ev("dout1"), b.ev("d")],
             inc="psy")
    except _StopSched:
        b.barrier("stop")
        zt = FL.reg("zout", DIM) if "zout" not in FL.regions else FL.at("zout")
        b.v(lambda v: v.memset(zt, 0.0), inc="pv")
        for t0 in range(NT):
            b.dma("sync", dout[t0 * 128:(t0 + 1) * 128, :], zt,
                  wait=[b.ev("pv")], sem="dout0")
        b.sy(lambda e: e.nop(), wait=[b.ev("dout0"), b.ev("d")], inc="psy")

    b.finalize()
    stack.close()
    return nc, (dbg_aps, b)



def _retile_k(wT, kchunks):
    n = wT.shape[1]
    return np.ascontiguousarray(wT.reshape(kchunks, 128, n).transpose(1, 0, 2))


def prep_host(L0_attn_x, L0_attn_W_in, L0_attn_conv_w, L0_attn_dt_bias,
              L0_attn_A_log, L0_attn_D, L0_attn_norm_w, L0_attn_W_out,
              L0_ffn_w1, L0_ffn_w2):
    W_in = np.asarray(L0_attn_W_in, dtype=np.float32)
    shared = {
        "wz_t": _retile_k(np.ascontiguousarray(W_in[0:HID].T), 16),
        "wbc_t": _retile_k(np.ascontiguousarray(W_in[HID:HID + CONV].T), 16),
        "wdt_t": _retile_k(np.ascontiguousarray(W_in[HID + CONV:].T), 16),
        "wout_t": _retile_k(np.ascontiguousarray(
            np.asarray(L0_attn_W_out, np.float32).T), 32),
        "w1_t": _retile_k(np.ascontiguousarray(
            np.asarray(L0_ffn_w1, np.float32).T), 16),
        "w2_t": _retile_k(np.ascontiguousarray(
            np.asarray(L0_ffn_w2, np.float32).T), 64),
        "conv_w": np.ascontiguousarray(
            np.asarray(L0_attn_conv_w, np.float32)
            .reshape(NPC, 128, KC).transpose(1, 0, 2).reshape(128, NPC * KC)),
        "dt_bias": np.asarray(L0_attn_dt_bias, np.float32).reshape(1, NH),
        "a_log": np.asarray(L0_attn_A_log, np.float32).reshape(1, NH),
        "dvec": np.asarray(L0_attn_D, np.float32).reshape(1, NH),
        "norm_w": np.ascontiguousarray(
            np.asarray(L0_attn_norm_w, np.float32).reshape(32, 128).T),
        "uincl": np.triu(np.ones((128, 128), np.float32)),
        "ustrict": np.tril(np.ones((128, 128), np.float32), -1),
        "eye": np.eye(128, dtype=np.float32),
    }
    x = np.asarray(L0_attn_x, np.float32)
    per_core = []
    for core in range(8):
        bsz, half = core // 2, core % 2
        x_own = np.ascontiguousarray(x[bsz, half * TH:(half + 1) * TH, :])
        if half == 1:
            x_halo = np.ascontiguousarray(x[bsz, 0:TH, :])
        else:
            x_halo = np.zeros((TH, DIM), np.float32)
        per_core.append({"x_own": x_own, "x_halo": x_halo})
    return shared, per_core


def kernel(**inputs):
    shared, per_core = prep_host(**inputs)
    nc, _ = build_program()
    in_maps = [dict(shared, **pc) for pc in per_core]
    res = run_bass_kernel_spmd(nc, in_maps, core_ids=list(range(8)))
    out = np.empty((4, 2048, DIM), np.float32)
    for core in range(8):
        bsz, half = core // 2, core % 2
        out[bsz, half * TH:(half + 1) * TH, :] = res.results[core]["out"]
    return out


# version 08
# speedup vs baseline: 3.0612x; 3.0612x over previous
import math
from contextlib import ExitStack

import numpy as np

import concourse.bass as bass
import concourse.mybir as mybir
from concourse.bass_utils import run_bass_kernel_spmd

F32 = mybir.dt.float32
BF16 = mybir.dt.bfloat16
FP16 = mybir.dt.float16
AF = mybir.ActivationFunctionType
ALU = mybir.AluOpType
AX = mybir.AxisListType

DIM = 2048
HID = 4096
NH = 64
PH = 64
NST = 128
KC = 4
Q = 128
CONV = HID + 2 * NST
NPC = CONV // 128
TH = 1024
NT = TH // 128
NCH = 8
EPS_LN = 1e-5
EPS_RMS = 1e-5

ENGS = ("sync", "gpsimd", "tensor", "vector", "scalar")


class Bld:
    def __init__(self, nc, debug=()):
        self.nc = nc
        self.debug = set(debug)
        self.steps = {e: [] for e in ENGS}
        self.semh = {}
        self.cnt = {}
        self.seen = {e: {} for e in ENGS}
        self._stack = None
        self.tag = "init"
        self.inst_tags = {}

    def mksem(self, name):
        self.semh[name] = self._stack.enter_context(self.nc.semaphore(name))
        self.cnt[name] = 0

    def _wait(self, eng_name, ev):
        if ev is None:
            return
        if isinstance(ev, list):
            for e in ev:
                self._wait(eng_name, e)
            return
        sem, val = ev
        if val <= 0:
            return
        if self.seen[eng_name].get(sem, 0) >= val:
            return
        self.seen[eng_name][sem] = val
        self.steps[eng_name].append(("w", sem, val))

    def ev(self, sem):
        return (sem, self.cnt[sem])

    OWN = {"vector": "pv", "scalar": "ps", "gpsimd": "pg",
           "tensor": "pt", "sync": "psy"}
    AUTO = {"vector", "scalar", "gpsimd"}

    def op(self, eng_name, fn, wait=None, inc=None, inc_by=1, auto=None):
        self._wait(eng_name, wait)
        self._lasttag = self.tag
        own = self.OWN[eng_name]
        auto = (eng_name in self.AUTO) if auto is None else auto
        if auto:
            if self.cnt[own] > 0:
                self.steps[eng_name].append(("w", own, self.cnt[own]))
                self.seen[eng_name][own] = max(
                    self.seen[eng_name].get(own, 0), self.cnt[own])
            if inc is None or inc == own:
                self.steps[eng_name].append(("i", fn, own, 1, self.tag))
                self.cnt[own] += 1
                return (own, self.cnt[own])
            self.steps[eng_name].append(("i2", fn, own, inc, inc_by, self.tag))
            self.cnt[own] += 1
            self.cnt[inc] += inc_by
            return (inc, self.cnt[inc])
        self.steps[eng_name].append(("i", fn, inc, inc_by, self.tag))
        if inc is not None:
            self.cnt[inc] += inc_by
            return (inc, self.cnt[inc])
        return None

    def sy(self, fn, wait=None, inc=None):
        return self.op("sync", fn, wait, inc)

    def g(self, fn, wait=None, inc=None):
        return self.op("gpsimd", fn, wait, inc)

    def t(self, fn, wait=None, inc=None):
        return self.op("tensor", fn, wait, inc)

    def v(self, fn, wait=None, inc=None):
        return self.op("vector", fn, wait, inc)

    def s(self, fn, wait=None, inc=None):
        return self.op("scalar", fn, wait, inc)

    def dma(self, eng_name, out, in_, wait=None, sem="d"):
        self._wait(eng_name, wait)

        def fn(eng, out=out, in_=in_):
            return eng.dma_start(out=out, in_=in_)

        self.steps[eng_name].append(("i", fn, sem, 16, self.tag))
        self.cnt[sem] += 16
        return (sem, self.cnt[sem])

    def barrier(self, tag=""):
        evs = []
        for e, sname in (("sync", "psy"), ("gpsimd", "pg"), ("tensor", "pt"),
                         ("vector", "pv"), ("scalar", "ps")):
            self.steps[e].append(("w", sname, self.cnt[sname]))
            evs.append(self.op(
                e, lambda eng: (eng.engine_nop() if hasattr(eng, "engine_nop")
                                else eng.nop()), inc=sname, auto=False))
        for e in ENGS:
            self._wait(e, evs)
            for sname in self.cnt:
                if sname not in ("psy", "pg", "pt", "pv", "ps"):
                    self._wait(e, (sname, self.cnt[sname]))

    def finalize(self):
        nc = self.nc
        semh = self.semh

        def run(eng, eng_name):
            for si, st in enumerate(self.steps[eng_name]):
                if st[0] == "w":
                    eng.wait_ge(semh[st[1]], st[2])
                elif st[0] == "i2":
                    _, fn, own, inc, inc_by, tag = st
                    ins = fn(eng)
                    self.inst_tags[ins.ins.name] = (tag, eng_name, si)
                    ins.then_inc(semh[own], 1)
                    ins.then_inc(semh[inc], inc_by)
                else:
                    _, fn, inc, inc_by, tag = st
                    ins = fn(eng)
                    if ins is not None and hasattr(ins, "ins"):
                        self.inst_tags[ins.ins.name] = (tag, eng_name, si)
                    if inc is not None:
                        assert ins is not None
                        ins.then_inc(semh[inc], inc_by)

        blk = self._stack.enter_context(nc.Block())

        @blk.sync
        def _(eng):
            run(eng, "sync")

        @blk.gpsimd
        def _(eng):
            run(eng, "gpsimd")

        @blk.tensor
        def _(eng):
            run(eng, "tensor")

        @blk.vector
        def _(eng):
            run(eng, "vector")

        @blk.scalar
        def _(eng):
            run(eng, "scalar")


class Alloc:

    def __init__(self, ap, n, name):
        self.ap = ap
        self.n = n
        self.name = name
        self.regions = {}

    def reg(self, name, n_elems):
        if name in self.regions:
            off0, n0 = self.regions[name]
            assert n0 == n_elems, (name, n0, n_elems)
            return self.ap[:, off0:off0 + n_elems]
        taken = sorted(self.regions.values())
        off = 0
        for o, n in taken:
            if off + n_elems <= o:
                break
            off = max(off, o + n)
        assert off + n_elems <= self.n, \
            f"{self.name} overflow: {name} needs {n_elems} at {off} (cap {self.n})"
        self.regions[name] = (off, n_elems)
        return self.ap[:, off:off + n_elems]

    def at(self, name):
        off, n = self.regions[name]
        return self.ap[:, off:off + n]

    def free(self, *names):
        for nm in names:
            self.regions.pop(nm, None)


class _StopSched(Exception):
    pass


def build_program(debug=(), upto=None, skips=()):
    nc = bass.Bass("TRN2", target_bir_lowering=False)
    b = Bld(nc, debug=debug)
    stack = ExitStack()
    b._stack = stack

    dx = {}

    def din(name, shape, dtype=F32):
        dx[name] = nc.dram_tensor(name, list(shape), dtype,
                                  kind="ExternalInput").ap()
        return dx[name]

    din("x_own", [TH, DIM])
    din("x_halo", [TH, DIM])
    din("wz_t", [128, 16, HID])
    din("wbc_t", [128, 16, CONV])
    din("wdt_t", [128, 16, NH])
    din("wout_t", [128, 32, DIM])
    din("w1_t", [128, 16, 8192])
    din("w2_t", [128, 64, DIM])
    din("conv_w", [128, NPC * KC])
    din("dt_bias", [1, NH])
    din("a_log", [1, NH])
    din("dvec", [1, NH])
    din("norm_w", [128, 32])
    din("uincl", [128, 128])
    din("ustrict", [128, 128])
    din("eye", [128, 128])
    dout = nc.dram_tensor("out", [TH, DIM], F32, kind="ExternalOutput").ap()
    d_silz = nc.dram_tensor("silz_s", [NCH, 128, HID], BF16, kind="Internal").ap()
    d_sum = nc.dram_tensor("sum_s", [16, 128, TH], BF16, kind="Internal").ap()

    dbg_aps = {}

    NBF = 72 * 1024
    NF = 14 * 1024
    abf = stack.enter_context(nc.sbuf_tensor("abf", [128, NBF], BF16))[:]
    af = stack.enter_context(nc.sbuf_tensor("af", [128, NF], F32))[:]
    a16 = stack.enter_context(nc.sbuf_tensor("a16", [128, 2560], FP16))[:]
    r_u16x = a16[:, 0:2048]
    r_u16h = a16[:, 2048:2560]
    BF = Alloc(abf, NBF, "bf16")
    FL = Alloc(af, NF, "f32")

    r_uincl_bf = BF.reg("uincl_bf", 128)
    r_ustrict_bf = BF.reg("ustrict_bf", 128)
    r_eye_bf = BF.reg("eye_bf", 128)
    r_onescol_bf = BF.reg("onescol_bf", 1)
    r_S = BF.reg("S", HID)

    r_uincl = FL.reg("uincl", 128)
    r_lstrict = FL.reg("lstrict", 128)
    r_eye_f = FL.reg("eye_f", 128)
    r_onescol = FL.reg("onescol", 1)
    r_ones1x = FL.reg("ones1x", 128)
    r_convw = FL.reg("convw", NPC * KC)
    r_normw = FL.reg("normw", 32)
    r_abc = FL.reg("abc", NH)
    r_dbc = FL.reg("dbc", NH)
    r_dtbbc = FL.reg("dtbbc", NH)
    r_cone = FL.reg("c_one", 1)
    r_ceps = FL.reg("c_eps", 1)
    r_cepsr = FL.reg("c_epsr", 1)
    r_cinvh = FL.reg("c_invh", 1)
    r_row = FL.reg("row64", 3 * NH)
    r_mcol = FL.reg("mcol", NT)
    r_rscol = FL.reg("rscol", NT)
    r_nmrs = FL.reg("nmrs", NT)
    r_amax = FL.reg("amax", NT)
    r_var = FL.reg("var", NT)
    r_beta = FL.reg("beta", NT)
    r_dq = FL.reg("dq", NT)
    r_nd384 = FL.reg("nd384", NT)
    r_wsum = FL.reg("wsum", 132)
    r_thr1 = FL.reg("thr1", 4)
    r_m12 = FL.reg("m12", 2)
    r_dtok = FL.reg("dtok", NT * NH)
    r_drow = FL.reg("drow", 512)
    r_gt = FL.reg("gt", 128)
    r_rn = FL.reg("rn", 8)

    pb = [stack.enter_context(nc.psum_tensor(f"pb{i}", [128, 512], F32))[:]
          for i in range(6)]
    pT = [stack.enter_context(nc.psum_tensor(f"pT{i}", [128, 128], BF16))[:]
          for i in range(2)]

    for sname in ("d", "dg", "gw0", "gw1", "dw0", "dw1", "dx0", "dx1",
                  "dsz0", "dsz1", "dzw0", "dzw1", "daw0", "daw1", "dar", "dar1",
                  "dout0", "dout1", "psy", "pg", "pt", "pv", "ps"):
        b.mksem(sname)

    def ckpt(name):
        if upto is not None and name == upto:
            raise _StopSched()

    def dump(name, ap, wait=None):
        if name not in b.debug:
            return
        sh = list(ap.shape)
        d = nc.dram_tensor("dbg_" + name, sh, ap.dtype,
                           kind="ExternalOutput").ap()
        dbg_aps[name] = d
        b.dma("sync", d, ap, wait=wait, sem="d")
        for e in ENGS:
            b._wait(e, ("d", b.cnt["d"]))

    try:
        b.dma("sync", r_uincl, dx["uincl"], sem="d")
        b.dma("sync", r_lstrict, dx["ustrict"], sem="d")
        b.dma("sync", r_eye_f, dx["eye"], sem="d")
        b.dma("sync", r_convw, dx["conv_w"], sem="d")
        b.dma("sync", r_normw, dx["norm_w"], sem="d")
        b.dma("gpsimd", r_uincl_bf, dx["uincl"], sem="dg")
        b.dma("gpsimd", r_ustrict_bf, dx["ustrict"], sem="dg")
        b.dma("gpsimd", r_eye_bf, dx["eye"], sem="dg")
        roff = FL.regions["row64"][0]
        b.dma("sync", af[0:1, roff:roff + NH], dx["dt_bias"], sem="d")
        b.dma("sync", af[0:1, roff + NH:roff + 2 * NH], dx["a_log"], sem="d")
        b.dma("sync", af[0:1, roff + 2 * NH:roff + 3 * NH], dx["dvec"], sem="d")

        b.v(lambda v: v.memset(r_onescol, 1.0), inc="pv")
        b.v(lambda v: v.memset(r_ones1x[0:1, :], 1.0), inc="pv")
        b.v(lambda v: v.memset(r_onescol_bf, 1.0), inc="pv")
        b.v(lambda v: v.memset(r_cone, 1.0), inc="pv")
        b.v(lambda v: v.memset(r_ceps, EPS_LN), inc="pv")
        b.v(lambda v: v.memset(r_cepsr, EPS_RMS), inc="pv")
        b.v(lambda v: v.memset(r_cinvh, 1.0 / HID), inc="pv")
        ev_vm = b.v(lambda v: v.memset(r_S, 0.0), inc="pv")

        evm = b.t(lambda t: t.matmul(pb[5][:, 0:NH], r_ones1x[0:1, :],
                                     af[0:1, roff:roff + NH], start=True, stop=True),
                  wait=[b.ev("d"), b.ev("dg"), ev_vm], inc="pt")
        ev1 = b.v(lambda v: v.tensor_copy(r_dtbbc, pb[5][:, 0:NH]), wait=evm, inc="pv")
        evm = b.t(lambda t: t.matmul(pb[5][:, 0:NH], r_ones1x[0:1, :],
                                     af[0:1, roff + NH:roff + 2 * NH],
                                     start=True, stop=True), wait=ev1, inc="pt")
        ev2 = b.s(lambda s: s.activation(r_abc, pb[5][:, 0:NH], AF.Exp),
                  wait=evm, inc="ps")
        ev2b = b.v(lambda v: v.tensor_scalar(r_abc, r_abc, -1.0, None, ALU.mult),
                   wait=ev2, inc="pv")
        evm = b.t(lambda t: t.matmul(pb[5][:, 0:NH], r_ones1x[0:1, :],
                                     af[0:1, roff + 2 * NH:roff + 3 * NH],
                                     start=True, stop=True), wait=ev2b, inc="pt")
        b.v(lambda v: v.tensor_copy(r_dbc, pb[5][:, 0:NH]), wait=evm, inc="pv")
        b.barrier("consts")

        wab = BF.reg("wabs_sc", 2048)
        wfl_ab = FL.reg("wabs_f", 2 * 2048)
        wflv_ab = wfl_ab.rearrange("p (b n) -> p b n", b=2)
        _wst = {"ev_use": [None, None], "ncol": 0}

        def _wabs_step(i):
            pp = i % 2
            ncol = _wst["ncol"]
            if i < 64:
                k, hcol = i // 4, (i % 4) * 2048
                src = dx["w1_t"][:, k:k + 1, hcol:hcol + 2048]
            else:
                j = i - 64
                src = dx["w2_t"][:, j:j + 1, :]
            evl = b.dma("sync", wflv_ab[:, pp, :], src, wait=_wst["ev_use"][pp],
                        sem="dw0" if pp == 0 else "dw1")
            _wst["ev_use"][pp] = b.s(lambda s, pp=pp, ncol=ncol: s.activation(
                wab, wflv_ab[:, pp, :], AF.Abs,
                accum_out=r_wsum[:, ncol:ncol + 1]), wait=evl, inc="ps")
            _wst["ncol"] = ncol + 1

        wabs_steps = [(lambda i=i: _wabs_step(i)) for i in range(128)]

        def wabs_pump(n=1):
            for _ in range(n):
                if wabs_steps:
                    wabs_steps.pop(0)()

        def wabs_reduce():
            ev_abs = [e for e in _wst["ev_use"] if e is not None]
            ev_r1 = b.v(lambda v: v.reduce_sum(
                r_wsum[:, 128:129], r_wsum[:, 0:64], AX.X),
                wait=ev_abs, inc="pv")
            ev_r2 = b.v(lambda v: v.reduce_sum(
                r_wsum[:, 129:130], r_wsum[:, 64:128], AX.X),
                wait=ev_abs, inc="pv")
            evm1 = b.t(lambda t: t.matmul(pb[5][0:1, 0:1], r_wsum[:, 128:129],
                                          r_onescol, start=True, stop=True),
                       wait=ev_r1, inc="pt")
            woff = FL.regions["wsum"][0]
            ev_s1 = b.v(lambda v: v.tensor_scalar(
                af[0:1, woff + 130:woff + 131],
                pb[5][0:1, 0:1], 1.0 / (8192.0 * DIM), 1e-5, ALU.mult, ALU.max),
                wait=evm1, inc="pv")
            evm2 = b.t(lambda t: t.matmul(pb[5][0:1, 1:2], r_wsum[:, 129:130],
                                          r_onescol, start=True, stop=True),
                       wait=[ev_r2, ev_s1], inc="pt")
            ev_s2 = b.v(lambda v: v.tensor_scalar(
                af[0:1, woff + 131:woff + 132],
                pb[5][0:1, 1:2], 1.0 / (8192.0 * DIM), 1e-5, ALU.mult, ALU.max),
                wait=evm2, inc="pv")
            evb = b.t(lambda t: t.matmul(pb[5][:, 2:4], r_ones1x[0:1, :],
                                         af[0:1, woff + 130:woff + 132],
                                         start=True, stop=True),
                      wait=ev_s2, inc="pt")
            b.v(lambda v: v.tensor_copy(r_m12, pb[5][:, 2:4]), wait=evb, inc="pv")
            b.v(lambda v: v.tensor_scalar(r_thr1[:, 0:1], r_m12[:, 0:1], 0.5,
                                          None, ALU.mult), inc="pv")
            b.v(lambda v: v.tensor_scalar(r_thr1[:, 1:2], r_m12[:, 0:1], -0.5,
                                          None, ALU.mult), inc="pv")
            b.v(lambda v: v.tensor_scalar(r_thr1[:, 2:3], r_m12[:, 1:2], 0.5,
                                          None, ALU.mult), inc="pv")
            b.v(lambda v: v.tensor_scalar(r_thr1[:, 3:4], r_m12[:, 1:2], -0.5,
                                          None, ALU.mult), inc="pv")
            BF.free("wabs_sc")
            FL.free("wabs_f")

        _tb_ev = {}

        def pe_T(dst_ap, src_ap, bf=True, bank=0, wait=None, scale_col=None,
                 add_ap=None, out_f32=False, eng="v"):
            key = ("b" if bf else "f", bank)
            ident = r_eye_bf if bf else r_eye_f
            pbank = pT[bank] if bf else pb[4 + bank][:, 0:128]
            w = [] if wait is None else [wait]
            if _tb_ev.get(key) is not None:
                w.append(_tb_ev[key])
            evt = b.t(lambda t: t.transpose(pbank, src_ap, ident), wait=w, inc="pt")
            if add_ap is not None:
                eve = b.v(lambda v: v.scalar_tensor_tensor(
                    dst_ap, pbank, 1.0, add_ap, ALU.mult, ALU.add),
                    wait=evt, inc="pv")
            elif eng == "s":
                if scale_col is not None:
                    eve = b.s(lambda s: s.activation(dst_ap, pbank, AF.Identity,
                                                     scale=scale_col),
                              wait=evt, inc="ps")
                else:
                    eve = b.s(lambda s: s.activation(dst_ap, pbank, AF.Identity),
                              wait=evt, inc="ps")
            elif scale_col is not None:
                eve = b.v(lambda v: v.tensor_scalar(dst_ap, pbank, scale_col,
                                                    None, ALU.mult),
                          wait=evt, inc="pv")
            else:
                eve = b.v(lambda v: v.tensor_copy(dst_ap, pbank), wait=evt, inc="pv")
            _tb_ev[key] = eve
            return eve

        def run_phase(ph):
            is_m = ph == "M"
            xdram = dx["x_own"] if is_m else dx["x_halo"]

            b.tag = f"ln{ph}"
            xnT_r = BF.reg("xnT", 16 * TH)
            xnT = xnT_r.rearrange("p (k n) -> p k n", k=16)
            xt2 = FL.reg("xtile", 2 * DIM).rearrange("p (b n) -> p b n", b=2)
            xnsc = BF.reg("xnt_sc", 2 * DIM).rearrange("p (b n) -> p b n", b=2)
            sqsc = BF.reg("sq_sc", DIM)
            colt = FL.reg("colt", 8)
            ev_xt = [None, None]
            ev_xn = [None, None]
            for t0 in range(NT):
                pp = t0 % 2
                xt = xt2[:, pp, :]
                xnt = xnsc[:, pp, :]
                evl = b.dma("sync", xt, xdram[t0 * 128:(t0 + 1) * 128, :],
                            wait=ev_xt[pp], sem="dx0" if pp == 0 else "dx1")
                evq = b.s(lambda s, xt=xt: s.activation(
                    sqsc, xt, AF.Square, accum_out=colt[:, 0:1]),
                    wait=evl, inc="ps")
                evsum = b.v(lambda v, xt=xt: v.reduce_sum(colt[:, 1:2], xt, AX.X),
                            wait=evl, inc="pv")
                mcol = r_mcol[:, t0:t0 + 1]
                vcol = r_var[:, t0:t0 + 1]
                rscol = r_rscol[:, t0:t0 + 1]
                nmrs = r_nmrs[:, t0:t0 + 1]
                b.v(lambda v, mcol=mcol: v.tensor_scalar(
                    mcol, colt[:, 1:2], 1.0 / DIM, None, ALU.mult), wait=evsum)
                b.v(lambda v, mcol=mcol: v.tensor_tensor(
                    colt[:, 2:3], mcol, mcol, ALU.mult), wait=evq)
                b.v(lambda v: v.tensor_scalar(
                    colt[:, 3:4], colt[:, 0:1], 1.0 / DIM, None, ALU.mult))
                b.v(lambda v, vcol=vcol: v.tensor_tensor(
                    vcol, colt[:, 3:4], colt[:, 2:3], ALU.subtract))
                evsq = b.s(lambda s, vcol=vcol: s.activation(
                    colt[:, 4:5], vcol, AF.Sqrt, bias=r_ceps, scale=1.0),
                    wait=b.ev("pv"), inc="ps")
                evrs = b.v(lambda v, rscol=rscol: v.reciprocal(
                    rscol, colt[:, 4:5]), wait=evsq, inc="pv")
                evn = b.v(lambda v, mcol=mcol: v.tensor_scalar(
                    colt[:, 5:6], mcol, -1.0, None, ALU.mult), wait=evrs, inc="pv")
                evn2 = b.v(lambda v, nmrs=nmrs, rscol=rscol: v.tensor_tensor(
                    nmrs, colt[:, 5:6], rscol, ALU.mult), wait=evn, inc="pv")
                evxn = b.s(lambda s, xnt=xnt, xt=xt, rscol=rscol, nmrs=nmrs:
                           s.activation(xnt, xt, AF.Identity, bias=nmrs, scale=rscol),
                           wait=evn2, inc="ps")
                ev_xt[pp] = [evxn, evsum]
                if is_m:
                    b.v(lambda v, xnt=xnt, t0=t0: v.reduce_max(
                        r_amax[:, t0:t0 + 1], xnt, AX.X,
                        apply_absolute_value=True), wait=evxn, inc="pv")
                for k in range(16):
                    pe_T(xnT[:, k, t0 * 128:(t0 + 1) * 128],
                         xnt[:, k * 128:(k + 1) * 128],
                         bf=True, bank=k % 2, wait=evxn)
                ev_xn[pp] = b.ev("pv")
                ev_xt[pp].append(ev_xn[pp])
                wabs_pump(2)
            FL.free("xtile")
            BF.free("xnt_sc", "sq_sc")
            FL.free("colt")
            dump(f"xnT_{ph}", xnT_r, wait=[b.ev("pv")])
            b.barrier(f"ln{ph}")
            ckpt(f"ln{ph}")

            b.tag = "zstream"
            if is_m:
                wfl0dbg = FL.reg("zdbg", 256) if "zps0" in b.debug else None
                wzpan = [BF.reg("wpA", 8192).rearrange("p (k n) -> p k n", k=16),
                        BF.reg("wpB", 8192).rearrange("p (k n) -> p k n", k=16)]
                ztmp = [BF.reg("ztA", 512), BF.reg("ztB", 512)]
                ev_wu = [None, None]
                ev_zt = [None, None]
                for p in range(8):
                    pp = p % 2
                    evw = b.dma("gpsimd", wzpan[pp],
                                dx["wz_t"][:, :, p * 512:(p + 1) * 512],
                                wait=ev_wu[pp], sem="gw0" if pp == 0 else "gw1")
                    if p == 0 and "wzp0" in b.debug:
                        dump("wzp0", wzpan[0].rearrange("p a n -> p (a n)"),
                             wait=[evw])
                    for t0 in range(NT):
                        bnk = t0 % 2
                        evm = None
                        for k in range(16):
                            evm = b.t(lambda t, k=k, t0=t0, pp=pp, bnk=bnk:
                                      t.matmul(pb[bnk][:, 0:512],
                                               xnT[:, k, t0 * 128:(t0 + 1) * 128],
                                               wzpan[pp][:, k, :],
                                               start=(k == 0), stop=(k == 15)),
                                      wait=[evw, ev_zt[bnk]] if k == 0 else None,
                                      inc="pt" if k == 15 else None)
                        zpp = t0 % 2
                        if p == 0 and t0 == 0 and "zps0" in b.debug:
                            evdbg = b.v(lambda v: v.tensor_copy(
                                wfl0dbg, pb[0][:, 0:256]), wait=evm, inc="pv")
                            dump("zps0", wfl0dbg, wait=[evdbg])
                            dump("xnT_z", xnT.rearrange("p a n -> p (a n)"),
                                 wait=[evdbg])
                        evsl = b.s(lambda s, bnk=bnk, zpp=zpp: s.activation(
                            ztmp[zpp], pb[bnk][:, 0:512], AF.Silu),
                            wait=evm, inc="ps")
                        evzw = b.dma("sync", d_silz[t0, :, p * 512:(p + 1) * 512],
                                     ztmp[zpp], wait=evsl,
                                     sem="dzw0" if t0 % 2 == 0 else "dzw1")
                        ev_zt[bnk] = [evsl, evzw]
                    ev_wu[pp] = b.ev("ps")
                BF.free("ztA", "ztB", "wpA", "wpB")
                b.barrier("zstream")
                ckpt("z")

            b.tag = f"dt{ph}"
            wdtp = BF.reg("wdtp", 1024).rearrange("p (k n) -> p k n", k=16)
            evw = b.dma("gpsimd", wdtp, dx["wdt_t"], sem="gw0")
            dtraw = FL.reg("dtraw", NT * NH).rearrange("p (a n) -> p a n", a=NT)
            ev_pb = None
            for t0 in range(NT):
                evm = None
                for k in range(16):
                    evm = b.t(lambda t, k=k, t0=t0: t.matmul(
                        pb[5][:, 0:NH], xnT[:, k, t0 * 128:(t0 + 1) * 128],
                        wdtp[:, k, :], start=(k == 0), stop=(k == 15)),
                        wait=[evw, ev_pb] if k == 0 else None,
                        inc="pt" if k == 15 else None)
                ev_pb = b.v(lambda v, t0=t0: v.tensor_copy(
                    dtraw[:, t0, :], pb[5][:, 0:NH]), wait=evm, inc="pv")
            dtv = r_dtok.rearrange("p (a n) -> p a n", a=NT)
            dtb_bc = r_dtbbc.unsqueeze(1).broadcast_to([128, NT, NH])
            eva = b.v(lambda v: v.tensor_tensor(
                dtv, dtraw, dtb_bc, ALU.add), wait=b.ev("pv"), inc="pv")
            eve = b.s(lambda s: s.activation(r_dtok, r_dtok, AF.Exp),
                      wait=eva, inc="ps")
            evl = b.s(lambda s: s.activation(r_dtok, r_dtok, AF.Ln, bias=r_cone),
                      wait=eve, inc="ps")
            FL.free("dtraw")
            BF.free("wdtp")
            dump(f"dtok_{ph}", r_dtok, wait=[evl])
            b.barrier(f"dt{ph}")
            ckpt(f"dt{ph}")

            b.tag = f"xbc{ph}"
            bigr = BF.reg("bigbuf", NPC * TH)
            big = bigr.rearrange("p (c n) -> p c n", c=NPC)
            wpan = [BF.reg("wpA", 4096).rearrange("p (k n) -> p k n", k=16),
                    BF.reg("wpB", 4096).rearrange("p (k n) -> p k n", k=16)]
            raw = BF.reg("raw", 2 * (TH + 3)).rearrange(
                "p (c n) -> p c n", c=2)
            cacc = FL.reg("cacA", 2 * TH).rearrange("p (c n) -> p c n", c=2)
            ctmp = FL.reg("cacB", 2 * TH).rearrange("p (c n) -> p c n", c=2)
            if ph == "H":
                haloraw = BF.reg("haloraw", NPC * 3).rearrange(
                    "p (c n) -> p c n", c=NPC)
            else:
                haloraw = BF.at("haloraw").rearrange("p (c n) -> p c n", c=NPC)
            cwv = r_convw.rearrange("p (c k) -> p c k", k=KC)
            panels = [(i * 2, 2) for i in range(17)]
            ev_wu = [None, None]
            ev_raw = None
            ev_silu = None
            for ip, (pc0, npc) in enumerate(panels):
                pp = ip % 2
                wsem = "dw0" if pp == 0 else "dw1"
                evw = b.dma("gpsimd", wpan[pp][:, :, 0:npc * 128],
                            dx["wbc_t"][:, :, pc0 * 128:(pc0 + npc) * 128],
                            wait=ev_wu[pp], sem="gw0" if pp == 0 else "gw1")
                ev_ev = []
                for n in range(2):
                    for jb in range(npc):
                        evm = None
                        for k in range(16):
                            evm = b.t(lambda t, k=k, jb=jb, n=n, pp=pp: t.matmul(
                                pb[jb],
                                wpan[pp][:, k, jb * 128:(jb + 1) * 128],
                                xnT[:, k, n * 512:(n + 1) * 512],
                                start=(k == 0), stop=(k == 15)),
                                wait=([evw, ev_raw] + ev_ev[-npc:]
                                      if k == 0 else None),
                                inc="pt" if k == 15 else None)
                        ev_ev.append(b.v(lambda v, jb=jb, n=n: v.tensor_copy(
                            raw[:, jb, 3 + n * 512:3 + (n + 1) * 512],
                            pb[jb]), wait=evm, inc="pv"))
                if ph == "H":
                    evp = b.v(lambda v, npc=npc: v.memset(raw[:, 0:npc, 0:3], 0.0),
                              inc="pv")
                else:
                    evp = b.v(lambda v, pc0=pc0, npc=npc: v.tensor_copy(
                        raw[:, 0:npc, 0:3], haloraw[:, pc0:pc0 + npc, :]),
                        inc="pv")
                evc = b.v(lambda v, npc=npc, pc0=pc0: v.tensor_tensor(
                    cacc[:, 0:npc, :], raw[:, 0:npc, 0:TH],
                    cwv[:, pc0:pc0 + npc, 0:1].broadcast_to([128, npc, TH]),
                    ALU.mult), wait=ev_ev + [evp, ev_silu], inc="pv")
                for k in range(1, KC):
                    b.v(lambda v, k=k, npc=npc, pc0=pc0: v.tensor_tensor(
                        ctmp[:, 0:npc, :], raw[:, 0:npc, k:k + TH],
                        cwv[:, pc0:pc0 + npc, k:k + 1].broadcast_to([128, npc, TH]),
                        ALU.mult), inc="pv")
                    evc = b.v(lambda v, npc=npc: v.tensor_tensor(
                        cacc[:, 0:npc, :], cacc[:, 0:npc, :], ctmp[:, 0:npc, :],
                        ALU.add), inc="pv")
                if ph == "H":
                    b.v(lambda v, pc0=pc0, npc=npc: v.tensor_copy(
                        haloraw[:, pc0:pc0 + npc, :], raw[:, 0:npc, TH:TH + 3]),
                        inc="pv")
                ev_raw = b.ev("pv")
                evs = b.s(lambda s, pc0=pc0, npc=npc: s.activation(
                    big[:, pc0:pc0 + npc, :], cacc[:, 0:npc, :], AF.Silu),
                    wait=ev_raw, inc="ps")
                ev_silu = evs
                ev_wu[pp] = evs
                wabs_pump(6)
            BF.free("raw")
            FL.free("cacA", "cacB")
            BF.free("wpA", "wpB")
            BF.free("xnT")
            b.barrier(f"conv{ph}")
            ckpt(f"conv{ph}")
            dump(f"conv_{ph}", bigr, wait=[b.ev("ps")])

            b.tag = f"chunks{ph}"
            xs_r = BF.reg("xs", 32 * 128)
            xs = xs_r.rearrange("p (c n) -> p c n", c=32)
            btok = BF.reg("btok", 128)
            bw8 = BF.reg("bw", 8 * 128).rearrange("p (c n) -> p c n", c=8)
            aU = BF.reg("aU", 8 * 128).rearrange("p (c n) -> p c n", c=8)
            Lg = BF.reg("Lg", 1024)
            Lgv = Lg.rearrange("p (a n) -> p a n", a=8)
            Mt = BF.reg("Mt", 8 * 128).rearrange("p (c n) -> p c n", c=8)
            ev1t = BF.reg("ev1t", 512)
            ev2t = BF.reg("ev2t", 512)
            ych = BF.reg("ych", HID)
            sq2 = BF.reg("sq2", HID)
            szb = [BF.reg("szA", HID), BF.reg("szB", HID)] if is_m else None

            ckall = FL.reg("ckall", 4 * 512)
            a_v = ckall[:, 0:512].rearrange("p (c n) -> p c n", c=8)
            cum_all = ckall[:, 512:1024]
            wdec_all = ckall[:, 1024:1536]
            wdec_v = wdec_all.rearrange("p (c n) -> p c n", c=8)
            dec_v = ckall[:, 1536:2048].rearrange("p (c n) -> p c n", c=8)
            ebf_r = BF.reg("ebfall", 512) if is_m else None
            ebf_v = (ebf_r.rearrange("p (c n) -> p c n", c=8)
                     if is_m else None)
            eva = b.v(lambda v: v.tensor_tensor(
                a_v, dtv, r_abc.unsqueeze(1).broadcast_to([128, NT, NH]),
                ALU.mult), inc="pv")
            evcm = b.t(lambda t: t.matmul(pb[5][:, 0:512], r_uincl,
                                          ckall[:, 0:512],
                                          start=True, stop=True),
                       wait=eva, inc="pt")
            evcum = b.v(lambda v: v.tensor_copy(cum_all, pb[5][:, 0:512]),
                        wait=evcm, inc="pv")
            evwm = b.t(lambda t: t.matmul(pb[5][:, 0:512], r_lstrict,
                                          ckall[:, 0:512],
                                          start=True, stop=True),
                       wait=evcum, inc="pt")
            evwe = b.s(lambda s: s.activation(wdec_all, pb[5][:, 0:512],
                                              AF.Exp), wait=evwm, inc="ps")
            evwd = b.v(lambda v: v.tensor_tensor(
                wdec_all, wdec_all, r_dtok, ALU.mult), wait=evwe, inc="pv")
            evdm = b.t(lambda t: t.matmul(pb[5][0:1, 0:512], r_onescol,
                                          ckall[:, 0:512],
                                          start=True, stop=True),
                       wait=evwe, inc="pt")
            evde = b.s(lambda s: s.activation(r_drow[0:1, :],
                                              pb[5][0:1, 0:512],
                                              AF.Exp), wait=evdm, inc="ps")
            evdb = b.t(lambda t: t.matmul(pb[5][:, 0:512],
                                          r_ones1x[0:1, :], r_drow[0:1, :],
                                          start=True, stop=True),
                       wait=evde, inc="pt")
            evdbc = b.v(lambda v: v.tensor_copy(ckall[:, 1536:2048],
                                                pb[5][:, 0:512]),
                        wait=evdb, inc="pv")
            eve_all = (b.s(lambda s: s.activation(
                ebf_r, cum_all, AF.Exp), wait=evcum, inc="ps")
                if is_m else None)

            dump(f"ckall_{ph}", ckall, wait=[evdbc, evwd, b.ev("ps")])
            ev_sz = [None, None]
            ev_szu = [None, None]
            ev_scan_prev = None
            ev_gate_prev = None
            ev_bwmm = None
            ev_auE = None
            ev_ynT_prev = None
            ev_gtprev = evdbc
            ev_sread = [evdbc, evdbc]
            nch_eff = 1 if (is_m and "c1" in skips) else NCH
            for c in range(nch_eff):
                cw = slice(c * 128, (c + 1) * 128)
                a_c = a_v[:, c, :]
                if is_m and "sz" not in skips:
                    pp = c % 2
                    ev_sz[pp] = b.dma("sync", szb[pp], d_silz[c, :, :],
                                      wait=[ev_szu[pp], b.ev("dzw0"),
                                            b.ev("dzw1")], sem=f"dsz{pp}")
                evT = []
                for m in range(32):
                    evT.append(pe_T(xs[:, m, :], big[:, m, cw], bf=True,
                                    bank=m % 2, eng="s", wait=ev_gate_prev))
                evB = pe_T(btok, big[:, 32, cw], bf=True, bank=0, eng="s",
                           wait=ev_bwmm)
                if is_m and "egt" not in skips:
                    evg = b.t(lambda t, cw=cw: t.matmul(
                        pb[5][:, 256:384], big[:, 32, cw], big[:, 33, cw],
                        start=True, stop=True), wait=[ev_gtprev],
                        inc="pt")
                    evgm = b.v(lambda v: v.tensor_tensor(
                        r_gt, pb[5][:, 256:384], r_uincl, ALU.mult),
                        wait=evg, inc="pv")
                    ev_gtprev = evgm
                ev_gprev = None
                for g in range(8):
                    hs = slice(g * 8 * PH, (g + 1) * 8 * PH)
                    if is_m and "inner" not in skips:
                        evau = [b.v(lambda v, g=g, a_c=a_c: v.tensor_tensor(
                            aU,
                            r_uincl_bf.unsqueeze(1).broadcast_to([128, 8, 128]),
                            a_c[:, g * 8:(g + 1) * 8].unsqueeze(2)
                            .broadcast_to([128, 8, 128]),
                            ALU.mult), wait=[eva, ev_auE], inc="pv")]
                        evE = None
                        for h8 in range(8):
                            evE = b.t(lambda t, h8=h8: t.matmul(
                                pb[3 + h8 // 4][:, (h8 % 4) * 128:(h8 % 4 + 1) * 128],
                                r_ustrict_bf, aU[:, h8, :],
                                start=(h8 % 4 == 0), stop=(h8 % 4 == 3)),
                                wait=(evau + [ev_gprev] if h8 == 0 else None),
                                inc="pt" if h8 == 7 else None)
                        ev_auE = evE
                        evL1 = b.s(lambda s: s.activation(Lg[:, 0:512], pb[3],
                                                          AF.Exp), wait=evE, inc="ps")
                        evL2 = b.s(lambda s: s.activation(Lg[:, 512:1024], pb[4],
                                                          AF.Exp), wait=evE, inc="ps")
                        evMt1 = b.v(lambda v, g=g, c=c: v.tensor_tensor(
                            Mt, Lgv,
                            dtv[:, c, g * 8:(g + 1) * 8].unsqueeze(2)
                            .broadcast_to([128, 8, 128]),
                            ALU.mult), wait=[evL1, evL2], inc="pv")
                        evMt = [b.v(lambda v: v.tensor_tensor(
                            Mt, Mt,
                            r_gt.unsqueeze(1).broadcast_to([128, 8, 128]),
                            ALU.mult), wait=[evMt1, evgm], inc="pv")]
                        if c == 0 and g == 0 and "mt0" in b.debug:
                            dump("mt0", Mt.rearrange("p a n -> p (a n)"),
                                 wait=evMt)
                            dump("lg0", Lg, wait=evMt)
                            dump("gt0", r_gt, wait=evMt)
                            dump("au0", aU.rearrange("p a n -> p (a n)"),
                                 wait=evMt)
                        evYo = None
                        for h8 in range(8):
                            h = g * 8 + h8
                            evYo = b.t(lambda t, h=h, h8=h8, cw=cw: t.matmul(
                                pb[1][:, h8 * PH:(h8 + 1) * PH], big[:, 33, cw],
                                r_S[:, h * PH:(h + 1) * PH],
                                start=(h8 == 0), stop=(h8 == 7)),
                                wait=([ev_scan_prev, evgm, ev_gprev, b.ev("pv")]
                                      if h8 == 0 else None),
                                inc="pt" if h8 == 7 else None)
                        evYd = None
                        for h8 in range(8):
                            h = g * 8 + h8
                            xsl = xs_r[:, h * PH:(h + 1) * PH]
                            evYd = b.t(lambda t, h8=h8, xsl=xsl: t.matmul(
                                pb[0][:, h8 * PH:(h8 + 1) * PH], Mt[:, h8, :], xsl,
                                start=(h8 == 0), stop=(h8 == 7)),
                                wait=(evMt + evT + [evB, ev_gprev]
                                      if h8 == 0 else None),
                                inc="pt" if h8 == 7 else None)
                    evbw = b.v(lambda v, g=g, c=c: v.tensor_tensor(
                        bw8,
                        btok.unsqueeze(1).broadcast_to([128, 8, 128]),
                        wdec_v[:, c, g * 8:(g + 1) * 8].unsqueeze(2)
                        .broadcast_to([128, 8, 128]),
                        ALU.mult),
                        wait=([evwd, evB, ev_bwmm] + (evT if g == 0 else [])),
                        inc="pv")
                    sb = 2 if is_m else 2 + (g % 2)
                    evst = None
                    for h8 in range(8):
                        h = g * 8 + h8
                        xsl = xs_r[:, h * PH:(h + 1) * PH]
                        evst = b.t(lambda t, h8=h8, xsl=xsl, sb=sb: t.matmul(
                            pb[sb][:, h8 * PH:(h8 + 1) * PH], bw8[:, h8, :], xsl,
                            start=(h8 == 0), stop=(h8 == 7)),
                            wait=[evbw] + ([ev_sread[sb - 2]] if h8 == 0 else []),
                            inc="pt")
                    ev_bwmm = evst
                    if is_m and "inner" not in skips:
                        evv1 = b.v(lambda v, g=g, c=c: v.tensor_tensor(
                            ev1t, pb[1].rearrange("p (a n) -> p a n", a=8),
                            ebf_v[:, c, g * 8:(g + 1) * 8].unsqueeze(2)
                            .broadcast_to([128, 8, PH]),
                            ALU.mult), wait=[evYo, eve_all], inc="pv")
                        evv2 = b.v(lambda v, hs=hs: v.tensor_tensor(
                            ych[:, hs], ev1t, pb[0], ALU.add),
                            wait=[evv1, evYd, ev_ynT_prev], inc="pv")
                        evv3 = b.v(lambda v, g=g, hs=hs: v.tensor_tensor(
                            ev1t.rearrange("p (a e n) -> p a e n", a=4, e=2),
                            xs_r[:, hs].rearrange("p (a n) -> p a n", a=4),
                            r_dbc[:, g * 8:(g + 1) * 8]
                            .rearrange("p (a e) -> p a e", a=4).unsqueeze(3)
                            .broadcast_to([128, 4, 2, PH]),
                            ALU.mult), wait=evv2, inc="pv")
                        evv4 = b.v(lambda v, hs=hs: v.tensor_tensor(
                            ych[:, hs], ych[:, hs], ev1t, ALU.add),
                            wait=evv3, inc="pv")
                    if c == 0 and g == 0 and f"stbank_{ph}" in b.debug:
                        evdbg = b.v(lambda v, sb=sb: v.tensor_copy(
                            sq2[:, 0:512], pb[sb]), wait=[evst], inc="pv")
                        evdbg2 = b.v(lambda v: v.tensor_copy(
                            sq2[:, 512:640], bw8[:, 0, :]),
                            wait=[evdbg], inc="pv")
                        dump(f"stbank_{ph}", sq2[:, 0:640], wait=[evdbg2])
                    evs1 = b.v(lambda v, g=g, c=c, hs=hs: v.tensor_tensor(
                        ev2t, r_S[:, hs].rearrange("p (a n) -> p a n", a=8),
                        dec_v[:, c, g * 8:(g + 1) * 8].unsqueeze(2)
                        .broadcast_to([128, 8, PH]), ALU.mult),
                        wait=[evdbc, evst] + ([evYo] if (is_m and "inner" not in skips) else []), inc="pv")
                    _oldtag = b.tag
                    b.tag = f"SCAN_{ph}_{c}_{g}"
                    evs2 = b.v(lambda v, hs=hs, sb=sb: v.tensor_tensor(
                        r_S[:, hs], ev2t, pb[sb], ALU.add), wait=evs1, inc="pv")
                    b.tag = _oldtag
                    ev_sread[sb - 2] = evs2
                    if c == 0 and g == 0 and f"Sg0_{ph}" in b.debug:
                        dump(f"Sg0_{ph}", r_S[:, 0:512], wait=[evs2])
                    if c == 0 and g == 0 and f"ev2t_{ph}" in b.debug:
                        dump(f"ev2t_{ph}", ev2t, wait=[evs1])
                    ev_gprev = b.ev("pv")
                if is_m and c == 0 and "ych0" in b.debug:
                    dump("ych0", ych, wait=[b.ev("pv")])
                ev_scan_prev = b.ev("pv")
                if c == 0:
                    dump(f"S_c0_{ph}", r_S, wait=[ev_scan_prev])
                    dump(f"xs_c0_{ph}", xs_r, wait=[ev_scan_prev])
                    dump(f"btok_c0_{ph}", btok, wait=[ev_scan_prev])
                if is_m and "gate" not in skips:
                    pp = c % 2
                    if c == 0 and "sz0" in b.debug:
                        dump("sz0", szb[pp], wait=[ev_sz[pp]])
                    evyz = b.v(lambda v, pp=pp: v.tensor_tensor(
                        ych, ych, szb[pp], ALU.mult),
                        wait=[ev_sz[pp], ev_scan_prev], inc="pv")
                    ev_szu[pp] = evyz
                    evss = b.v(lambda v: v.scalar_tensor_tensor(
                        sq2, ych, 1.0, ych, ALU.mult, ALU.mult,
                        accum_out=r_rn[:, 0:1]), wait=evyz, inc="pv")
                    evsr = b.s(lambda s: s.activation(
                        r_rn[:, 1:2], r_rn[:, 0:1], AF.Sqrt, bias=r_cepsr,
                        scale=r_cinvh), wait=evss, inc="ps")
                    evrc = b.v(lambda v: v.reciprocal(r_rn[:, 2:3], r_rn[:, 1:2]),
                               wait=evsr, inc="pv")
                    evyn = b.v(lambda v: v.tensor_scalar(
                        ych, ych, r_rn[:, 2:3], None, ALU.mult),
                        wait=evrc, inc="pv")
                    if c == 0 and "yn0" in b.debug:
                        dump("yn0", ych, wait=[evyn])
                    for k in range(32):
                        ev_ynT_prev = pe_T(
                            big[:, k, cw], ych[:, k * 128:(k + 1) * 128],
                            bf=True, bank=k % 2, wait=evyn,
                            scale_col=r_normw[:, k:k + 1], eng="s")
                ev_gate_prev = b.ev("pv")
                wabs_pump(2)
            BF.free("xs", "btok", "bw", "aU", "Lg", "Mt",
                    "ev1t", "ev2t", "ych", "sq2")
            FL.free("ckall")
            if is_m:
                BF.free("szA", "szB", "ebfall")
            b.barrier(f"chunks{ph}")
            ckpt(f"chunks{ph}")
            if ph == "H":
                dump("S_mid", r_S, wait=[b.ev("pv")])
                BF.free("bigbuf")
            return

        run_phase("H")
        ckpt("H")
        wabs_pump(128)
        wabs_reduce()
        b.barrier("wabs")
        ckpt("wabs")
        run_phase("M")
        ckpt("M")
        dump("ynT", BF.at("bigbuf"), wait=[b.ev("pv")])

        big = BF.at("bigbuf").rearrange("p (c n) -> p c n", c=NPC)
        b.tag = "wout"
        wpo = [BF.reg("woA", 8192).rearrange("p (k n) -> p k n", k=32),
               BF.reg("woB", 8192).rearrange("p (k n) -> p k n", k=32)]
        atmp = [BF.reg("atA", 512), BF.reg("atB", 512)]
        ev_wu = [None, None]
        ev_at = [None, None]
        ev_bank = [None] * 4
        for p in range(8):
            pp = p % 2
            evw = b.dma("gpsimd", wpo[pp], dx["wout_t"][:, :, p * 256:(p + 1) * 256],
                        wait=ev_wu[pp], sem="gw0" if pp == 0 else "gw1")
            for jb in range(2):
                jbg = p * 2 + jb
                for n in range(2):
                    bnk = jb * 2 + n
                    evm = None
                    for k in range(32):
                        evm = b.t(lambda t, k=k, jb=jb, n=n, pp=pp, bnk=bnk:
                                  t.matmul(pb[bnk],
                                           wpo[pp][:, k, jb * 128:(jb + 1) * 128],
                                           big[:, k, n * 512:(n + 1) * 512],
                                           start=(k == 0), stop=(k == 31)),
                                  wait=[evw, ev_bank[bnk]] if k == 0 else None,
                                  inc="pt" if k == 31 else None)
                    app = (jbg * 2 + n) % 2
                    eva = b.v(lambda v, bnk=bnk, app=app: v.tensor_copy(
                        atmp[app], pb[bnk]), wait=[evm, ev_at[app]], inc="pv")
                    evd = b.dma("sync", d_sum[jbg, :, n * 512:(n + 1) * 512],
                                atmp[app], wait=eva, sem=f"daw{app}")
                    ev_at[app] = evd
                    ev_bank[bnk] = eva
            ev_wu[pp] = b.ev("pv")
        BF.free("woA", "woB", "atA", "atB", "bigbuf")
        b.barrier("wout")
        ckpt("wout")

        b.tag = "qcols"
        cq = FL.reg("colq", 8)
        evd1 = b.v(lambda v: v.tensor_scalar(cq[:, 0:8], r_var, float(DIM), None,
                                             ALU.mult), inc="pv")
        evd2 = b.s(lambda s: s.activation(r_beta, r_var, AF.Identity, bias=r_ceps,
                                          scale=1.0), wait=evd1, inc="ps")
        evd3 = b.v(lambda v: v.reciprocal(r_beta, r_beta), wait=evd2, inc="pv")
        evd4 = b.v(lambda v: v.tensor_tensor(cq[:, 0:8], cq[:, 0:8], r_beta,
                                             ALU.mult), wait=evd3, inc="pv")
        evd5 = b.s(lambda s: s.activation(cq[:, 0:8], cq[:, 0:8], AF.Sqrt),
                   wait=evd4, inc="ps")
        evd6 = b.v(lambda v: v.tensor_scalar(cq[:, 0:8], cq[:, 0:8], 1e-12, None,
                                             ALU.max), wait=evd5, inc="pv")
        evd7 = b.v(lambda v: v.reciprocal(r_beta, cq[:, 0:8]), wait=evd6, inc="pv")
        evd8 = b.v(lambda v: v.tensor_scalar(r_beta, r_beta, math.sqrt(DIM), None,
                                             ALU.mult), wait=evd7, inc="pv")
        evd9 = b.v(lambda v: v.tensor_tensor(cq[:, 0:8], r_beta, r_amax, ALU.mult),
                   wait=evd8, inc="pv")
        evd10 = b.v(lambda v: v.tensor_scalar(cq[:, 0:8], cq[:, 0:8], 1e-5, None,
                                              ALU.max), wait=evd9, inc="pv")
        evd11 = b.v(lambda v: v.reciprocal(r_dq, cq[:, 0:8]), wait=evd10, inc="pv")
        evd12 = b.v(lambda v: v.tensor_tensor(r_beta, r_beta, r_dq, ALU.mult),
                    wait=evd11, inc="pv")
        evd13 = b.v(lambda v: v.tensor_scalar(r_beta, r_beta, 127.0, None,
                                              ALU.mult), wait=evd12, inc="pv")
        evd14 = b.v(lambda v: v.tensor_scalar(r_dq, cq[:, 0:8], 1.0 / 127.0, None,
                                              ALU.mult), wait=evd13, inc="pv")
        evd15 = b.v(lambda v: v.tensor_scalar(r_dq, r_dq, r_m12[:, 0:1], None,
                                              ALU.mult), wait=evd14, inc="pv")
        evd16 = b.v(lambda v: v.tensor_scalar(r_nd384, r_dq, -1536.0, None,
                                              ALU.mult), wait=evd15, inc="pv")
        FL.free("colq")
        b.barrier("qcols")
        ckpt("qcols")

        b.tag = "ffn"
        hh_r = BF.reg("hh", 64 * 512)
        hh = hh_r.rearrange("p (c n) -> p c n", c=64)
        xqT = BF.reg("xqT", 16 * 512).rearrange("p (k n) -> p k n", k=16)
        wq = [BF.reg("wqA", 4096).rearrange("p (k n) -> p k n", k=16),
              BF.reg("wqB", 4096).rearrange("p (k n) -> p k n", k=16)]
        pos = BF.reg("pos", 4096)
        neg = BF.reg("neg", 4096)
        usc = BF.reg("usc", DIM)
        u2 = FL.reg("u2", 512)
        sqh = [FL.reg("sqhA", 512), FL.reg("sqhB", 512)]
        stmp = [BF.reg("stA", 512), BF.reg("stB", 512)]
        sc1 = FL.reg("sc1", 512)
        hrow = FL.reg("hrow", 512)
        hax = FL.reg("hax", 16)
        bbc = FL.reg("bbc", 512)
        dbc2 = FL.reg("dbc2", 512)

        for hf in range(2):
            tks = range(hf * 4, hf * 4 + 4)
            xt2 = FL.reg("xtile", 2 * DIM).rearrange("p (b n) -> p b n", b=2)
            ev_xt = [None, None]
            ev_usc = None
            for i, t0 in enumerate(tks):
                pp = i % 2
                xt = xt2[:, pp, :]
                evl = b.dma("sync", xt, dx["x_own"][t0 * 128:(t0 + 1) * 128, :],
                            wait=ev_xt[pp], sem="dx0" if pp == 0 else "dx1")
                evxn = b.s(lambda s, xt=xt, t0=t0: s.activation(
                    xt, xt, AF.Identity, bias=r_nmrs[:, t0:t0 + 1],
                    scale=r_rscol[:, t0:t0 + 1]), wait=[evl, ev_usc], inc="ps")
                evu = b.v(lambda v, xt=xt, t0=t0: v.tensor_scalar(
                    r_u16x, xt, r_beta[:, t0:t0 + 1], 1536.0, ALU.mult, ALU.add),
                    wait=evxn, inc="pv")
                evq = b.v(lambda v, t0=t0: v.tensor_scalar(
                    usc, r_u16x, r_dq[:, t0:t0 + 1], r_nd384[:, t0:t0 + 1],
                    ALU.mult, ALU.add), wait=evu, inc="pv")
                for k in range(16):
                    pe_T(xqT[:, k, i * 128:(i + 1) * 128],
                         usc[:, k * 128:(k + 1) * 128], bf=True, bank=k % 2,
                         wait=evq)
                ev_usc = b.ev("pv")
                ev_xt[pp] = b.ev("pv")
            b.barrier(f"xq{hf}")
            if hf == 0 and "xqT0" in b.debug:
                dump("xqT0", xqT.rearrange("p a n -> p (a n)"), wait=[b.ev("pv")])
            FL.free("xtile")
            wfl = FL.reg("wf", 2 * 4096).rearrange("p (b n) -> p b n", b=2)
            ev_wu = [None, None]
            ev_bank = [None] * 4
            for p in range(32):
                pp = p % 2
                evw = b.dma("sync", wfl[:, pp, :],
                            dx["w1_t"][:, :, p * 256:(p + 1) * 256],
                            wait=ev_wu[pp], sem="dw0" if pp == 0 else "dw1")
                evp_ = b.v(lambda v, pp=pp, wfl=wfl: v.tensor_scalar(
                    pos, wfl[:, pp, :], r_thr1[:, 0:1], None, ALU.is_ge),
                    wait=evw, inc="pv")
                evn_ = b.v(lambda v, pp=pp, wfl=wfl: v.tensor_scalar(
                    neg, wfl[:, pp, :], r_thr1[:, 1:2], None, ALU.is_le),
                    wait=evw, inc="pv")
                evq_ = b.v(lambda v, pp=pp: v.tensor_tensor(
                    wq[pp].rearrange("p a n -> p (a n)"), pos, neg, ALU.subtract),
                    wait=[evp_, evn_], inc="pv")
                ev_wu[pp] = evq_
                for jb in range(2):
                    jbg = p * 2 + jb
                    bnk = (p % 2) * 2 + jb
                    evm = None
                    for k in range(16):
                        evm = b.t(lambda t, k=k, jb=jb, pp=pp, bnk=bnk: t.matmul(
                            pb[bnk], wq[pp][:, k, jb * 128:(jb + 1) * 128],
                            xqT[:, k, :], start=(k == 0), stop=(k == 15)),
                            wait=[evq_, ev_bank[bnk]] if k == 0 else None,
                            inc="pt" if k == 15 else None)
                    ev_bank[bnk] = b.s(lambda s, jbg=jbg, bnk=bnk: s.activation(
                        hh[:, jbg, :], pb[bnk], AF.Gelu), wait=evm, inc="ps")
                    spp = jbg % 2
                    evsq = b.v(lambda v, jbg=jbg, spp=spp: v.tensor_tensor(
                        sqh[spp], hh[:, jbg, :], hh[:, jbg, :], ALU.mult),
                        wait=[ev_bank[bnk]], inc="pv")
                    if jbg == 0:
                        b.v(lambda v: v.tensor_copy(u2, sqh[0]),
                            wait=evsq, inc="pv")
                        b.v(lambda v: v.tensor_copy(sc1, sqh[0]), inc="pv")
                    else:
                        b.v(lambda v, spp=spp: v.tensor_tensor(
                            u2, u2, sqh[spp], ALU.max), wait=evsq, inc="pv")
                        b.v(lambda v, spp=spp: v.tensor_tensor(
                            sc1, sc1, sqh[spp], ALU.add), inc="pv")
                ev_wu[pp] = [evq_, ev_bank[(p % 2) * 2], ev_bank[(p % 2) * 2 + 1]]
            b.barrier(f"w1_{hf}")
            if hf == 0 and "hh0" in b.debug:
                dump("hh0", hh_r, wait=[b.ev("ps")])
            evn2m = b.t(lambda t: t.matmul(pb[4][0:1, 0:512], r_onescol, sc1,
                                           start=True, stop=True), inc="pt")
            evh2 = b.v(lambda v: v.tensor_copy(hrow[0:1, 0:512], pb[4][0:1, 0:512]),
                       wait=evn2m, inc="pv")
            ev_tb = [None, None]
            for i in range(4):
                pbk = pb[4] if i % 2 == 0 else pb[5]
                evt_ = b.t(lambda t, i=i, pbk=pbk: t.transpose(
                    pbk[:, 0:128], u2[:, i * 128:(i + 1) * 128], r_eye_f),
                    wait=[evh2, ev_tb[i % 2]], inc="pt")
                ev_tb[i % 2] = b.v(lambda v, i=i, pbk=pbk: v.reduce_max(
                    hax[:, i:i + 1], pbk[:, 0:128], AX.X,
                    apply_absolute_value=True), wait=evt_, inc="pv")
            evsx = b.s(lambda s: s.activation(hax[:, 0:4], hax[:, 0:4], AF.Sqrt),
                       wait=b.ev("pv"), inc="ps")
            for i in range(4):
                evt_ = b.t(lambda t, i=i: t.transpose(
                    pb[4][:, 0:1], hrow[0:1, i * 128:(i + 1) * 128],
                    r_ones1x[0:1, 0:1]), wait=[evh2, b.ev("pv"), evsx], inc="pt")
                b.v(lambda v, i=i: v.tensor_copy(hax[:, 4 + i:5 + i],
                                                 pb[4][:, 0:1]),
                    wait=evt_, inc="pv")
            evc1 = b.s(lambda s: s.activation(hax[:, 8:12], hax[:, 4:8], AF.Sqrt),
                       wait=b.ev("pv"), inc="ps")
            evc2 = b.v(lambda v: v.tensor_scalar(hax[:, 8:12], hax[:, 8:12],
                                                 1e-12, None, ALU.max),
                       wait=evc1, inc="pv")
            evc3 = b.v(lambda v: v.reciprocal(hax[:, 8:12], hax[:, 8:12]),
                       wait=evc2, inc="pv")
            evc4 = b.v(lambda v: v.tensor_scalar(hax[:, 8:12], hax[:, 8:12],
                                                 math.sqrt(8192.0), None, ALU.mult),
                       wait=evc3, inc="pv")
            evc5 = b.v(lambda v: v.tensor_tensor(hax[:, 12:16], hax[:, 8:12],
                                                 hax[:, 0:4], ALU.mult),
                       wait=[evc4, evsx], inc="pv")
            evc6 = b.v(lambda v: v.tensor_scalar(hax[:, 12:16], hax[:, 12:16],
                                                 1e-5, None, ALU.max),
                       wait=evc5, inc="pv")
            evc7 = b.v(lambda v: v.reciprocal(hax[:, 0:4], hax[:, 12:16]),
                       wait=evc6, inc="pv")
            evc8 = b.v(lambda v: v.tensor_tensor(hax[:, 8:12], hax[:, 8:12],
                                                 hax[:, 0:4], ALU.mult),
                       wait=evc7, inc="pv")
            evc9 = b.v(lambda v: v.tensor_scalar(hax[:, 8:12], hax[:, 8:12],
                                                 127.0, None, ALU.mult),
                       wait=evc8, inc="pv")
            evc10 = b.v(lambda v: v.tensor_scalar(hax[:, 12:16], hax[:, 12:16],
                                                  1.0 / 127.0, None, ALU.mult),
                        wait=evc9, inc="pv")
            evc11 = b.v(lambda v: v.tensor_scalar(hax[:, 12:16], hax[:, 12:16],
                                                  r_m12[:, 1:2], None, ALU.mult),
                        wait=evc10, inc="pv")
            for i in range(4):
                evt_ = b.t(lambda t, i=i: t.transpose(
                    pb[4][0:1, 0:128], hax[:, 8 + i:9 + i], r_eye_f),
                    wait=[evc11, b.ev("pv")], inc="pt")
                evr_ = b.v(lambda v: v.tensor_copy(hrow[0:1, 0:128],
                                                   pb[4][0:1, 0:128]),
                           wait=evt_, inc="pv")
                evb_ = b.t(lambda t: t.matmul(pb[5][:, 0:128], r_ones1x[0:1, :],
                                              hrow[0:1, 0:128],
                                              start=True, stop=True),
                           wait=evr_, inc="pt")
                b.v(lambda v, i=i: v.tensor_copy(bbc[:, i * 128:(i + 1) * 128],
                                                 pb[5][:, 0:128]),
                    wait=evb_, inc="pv")
                evt_ = b.t(lambda t, i=i: t.transpose(
                    pb[4][0:1, 0:128], hax[:, 12 + i:13 + i], r_eye_f),
                    wait=b.ev("pv"), inc="pt")
                evr_ = b.v(lambda v: v.tensor_copy(hrow[0:1, 0:128],
                                                   pb[4][0:1, 0:128]),
                           wait=evt_, inc="pv")
                evb_ = b.t(lambda t: t.matmul(pb[5][:, 0:128], r_ones1x[0:1, :],
                                              hrow[0:1, 0:128],
                                              start=True, stop=True),
                           wait=evr_, inc="pt")
                b.v(lambda v, i=i: v.tensor_copy(dbc2[:, i * 128:(i + 1) * 128],
                                                 pb[5][:, 0:128]),
                    wait=evb_, inc="pv")
            b.barrier(f"hstats{hf}")
            for j in range(64):
                ev1_ = b.v(lambda v, j=j: v.tensor_tensor(
                    sc1, hh[:, j, :], bbc, ALU.mult), inc="pv")
                ev2_ = b.v(lambda v: v.tensor_scalar(
                    r_u16h, sc1, 1536.0, None, ALU.add), wait=ev1_, inc="pv")
                b.v(lambda v, j=j: v.scalar_tensor_tensor(
                    hh[:, j, :], r_u16h, -1536.0, dbc2, ALU.add, ALU.mult),
                    wait=ev2_, inc="pv")
            if hf == 0 and "hq0" in b.debug:
                dump("hq0", hh_r, wait=[b.ev("pv")])
                dump("hax0", hax, wait=[b.ev("pv")])
                dump("bbc0", bbc, wait=[b.ev("pv")])
                dump("dbc0", dbc2, wait=[b.ev("pv")])
            b.barrier(f"hq{hf}")
            ev_wu = [None, None]
            ev_bank2 = [None] * 4
            ev_ar = [None, None]
            for p in range(8):
                for ks in range(4):
                    pp = (p * 4 + ks) % 2
                    evw = b.dma("sync", wfl[:, pp, :],
                                dx["w2_t"][:, ks * 16:(ks + 1) * 16,
                                           p * 256:(p + 1) * 256],
                                wait=ev_wu[pp], sem="dw0" if pp == 0 else "dw1")
                    evp_ = b.v(lambda v, pp=pp, wfl=wfl: v.tensor_scalar(
                        pos, wfl[:, pp, :], r_thr1[:, 2:3], None, ALU.is_ge),
                        wait=evw, inc="pv")
                    evn_ = b.v(lambda v, pp=pp, wfl=wfl: v.tensor_scalar(
                        neg, wfl[:, pp, :], r_thr1[:, 3:4], None, ALU.is_le),
                        wait=evw, inc="pv")
                    evq_ = b.v(lambda v, pp=pp: v.tensor_tensor(
                        wq[pp].rearrange("p a n -> p (a n)"), pos, neg,
                        ALU.subtract), wait=[evp_, evn_], inc="pv")
                    ev_wu[pp] = evq_
                    for jb in range(2):
                        bnk = (p % 2) * 2 + jb
                        evm = None
                        for k in range(16):
                            evm = b.t(lambda t, k=k, jb=jb, pp=pp, ks=ks, bnk=bnk:
                                      t.matmul(pb[bnk],
                                               wq[pp][:, k, jb * 128:(jb + 1) * 128],
                                               hh[:, ks * 16 + k, :],
                                               start=(ks == 0 and k == 0),
                                               stop=(ks == 3 and k == 15)),
                                      wait=([evq_, ev_bank2[bnk]]
                                            if k == 0 else None),
                                      inc="pt" if k == 15 else None)
                        ev_wu[pp] = [evq_, evm]
                        if ks == 3:
                            jbg = p * 2 + jb
                            app = jbg % 2
                            evar = b.dma("sync", stmp[app],
                                         d_sum[jbg, :, hf * 512:(hf + 1) * 512],
                                         wait=[b.ev("daw0"), b.ev("daw1"),
                                               ev_ar[app]],
                                         sem="dar" if app == 0 else "dar1")
                            evsum = b.v(lambda v, bnk=bnk, app=app:
                                        v.scalar_tensor_tensor(
                                            stmp[app], pb[bnk], 1.0, stmp[app],
                                            ALU.mult, ALU.add),
                                        wait=[evm, evar], inc="pv")
                            evwr = b.dma("sync",
                                         d_sum[jbg, :, hf * 512:(hf + 1) * 512],
                                         stmp[app], wait=evsum, sem=f"daw{app}")
                            ev_ar[app] = evsum
                            ev_bank2[bnk] = evsum
            b.barrier(f"w2_{hf}")
            FL.free("wf")
            ckpt(f"ffn{hf}")
        BF.free("hh", "xqT", "wqA", "wqB", "pos", "neg", "usc", "u2",
                "sqhA", "sqhB", "stA", "stB")
        FL.free("sc1", "hrow", "hax", "bbc", "dbc2")

        b.tag = "final"
        fins = [BF.reg("finA", 16 * 128).rearrange("p (k n) -> p k n", k=16),
                BF.reg("finB", 16 * 128).rearrange("p (k n) -> p k n", k=16)]
        xt2 = FL.reg("xtile", 2 * DIM).rearrange("p (b n) -> p b n", b=2)
        ot2 = FL.reg("otile", 2 * DIM).rearrange("p (b n) -> p b n", b=2)
        ev_fin = [None, None]
        ev_ot = [None, None]
        for t0 in range(NT):
            pp = t0 % 2
            evf = b.dma("sync", fins[pp],
                        d_sum[:, :, t0 * 128:(t0 + 1) * 128].transpose([1, 0, 2]),
                        wait=[ev_fin[pp], b.ev("daw0"), b.ev("daw1")],
                        sem="dar" if pp == 0 else "dar1")
            evx = b.dma("sync", xt2[:, pp, :],
                        dx["x_own"][t0 * 128:(t0 + 1) * 128, :],
                        wait=ev_ot[pp], sem="dx0" if pp == 0 else "dx1")
            evs = []
            for k in range(16):
                evs.append(pe_T(ot2[:, pp, k * 128:(k + 1) * 128],
                                fins[pp][:, k, :], bf=True, bank=k % 2,
                                wait=[evf, evx],
                                add_ap=xt2[:, pp, k * 128:(k + 1) * 128]))
            evo = b.dma("sync", dout[t0 * 128:(t0 + 1) * 128, :], ot2[:, pp, :],
                        wait=evs, sem="dout0" if pp == 0 else "dout1")
            ev_fin[pp] = evs
            ev_ot[pp] = evo
        b.sy(lambda e: e.nop(), wait=[b.ev("dout0"), b.ev("dout1"), b.ev("d")],
             inc="psy")
    except _StopSched:
        b.barrier("stop")
        zt = FL.reg("zout", DIM) if "zout" not in FL.regions else FL.at("zout")
        b.v(lambda v: v.memset(zt, 0.0), inc="pv")
        for t0 in range(NT):
            b.dma("sync", dout[t0 * 128:(t0 + 1) * 128, :], zt,
                  wait=[b.ev("pv")], sem="dout0")
        b.sy(lambda e: e.nop(), wait=[b.ev("dout0"), b.ev("d")], inc="psy")

    b.finalize()
    stack.close()
    return nc, (dbg_aps, b)



def _retile_k(wT, kchunks):
    n = wT.shape[1]
    return np.ascontiguousarray(wT.reshape(kchunks, 128, n).transpose(1, 0, 2))


def prep_host(L0_attn_x, L0_attn_W_in, L0_attn_conv_w, L0_attn_dt_bias,
              L0_attn_A_log, L0_attn_D, L0_attn_norm_w, L0_attn_W_out,
              L0_ffn_w1, L0_ffn_w2):
    W_in = np.asarray(L0_attn_W_in, dtype=np.float32)
    shared = {
        "wz_t": _retile_k(np.ascontiguousarray(W_in[0:HID].T), 16),
        "wbc_t": _retile_k(np.ascontiguousarray(W_in[HID:HID + CONV].T), 16),
        "wdt_t": _retile_k(np.ascontiguousarray(W_in[HID + CONV:].T), 16),
        "wout_t": _retile_k(np.ascontiguousarray(
            np.asarray(L0_attn_W_out, np.float32).T), 32),
        "w1_t": _retile_k(np.ascontiguousarray(
            np.asarray(L0_ffn_w1, np.float32).T), 16),
        "w2_t": _retile_k(np.ascontiguousarray(
            np.asarray(L0_ffn_w2, np.float32).T), 64),
        "conv_w": np.ascontiguousarray(
            np.asarray(L0_attn_conv_w, np.float32)
            .reshape(NPC, 128, KC).transpose(1, 0, 2).reshape(128, NPC * KC)),
        "dt_bias": np.asarray(L0_attn_dt_bias, np.float32).reshape(1, NH),
        "a_log": np.asarray(L0_attn_A_log, np.float32).reshape(1, NH),
        "dvec": np.asarray(L0_attn_D, np.float32).reshape(1, NH),
        "norm_w": np.ascontiguousarray(
            np.asarray(L0_attn_norm_w, np.float32).reshape(32, 128).T),
        "uincl": np.triu(np.ones((128, 128), np.float32)),
        "ustrict": np.tril(np.ones((128, 128), np.float32), -1),
        "eye": np.eye(128, dtype=np.float32),
    }
    x = np.asarray(L0_attn_x, np.float32)
    per_core = []
    for core in range(8):
        bsz, half = core // 2, core % 2
        x_own = np.ascontiguousarray(x[bsz, half * TH:(half + 1) * TH, :])
        if half == 1:
            x_halo = np.ascontiguousarray(x[bsz, 0:TH, :])
        else:
            x_halo = np.zeros((TH, DIM), np.float32)
        per_core.append({"x_own": x_own, "x_halo": x_halo})
    return shared, per_core


def kernel(**inputs):
    shared, per_core = prep_host(**inputs)
    nc, _ = build_program()
    in_maps = [dict(shared, **pc) for pc in per_core]
    res = run_bass_kernel_spmd(nc, in_maps, core_ids=list(range(8)))
    out = np.empty((4, 2048, DIM), np.float32)
    for core in range(8):
        bsz, half = core // 2, core % 2
        out[bsz, half * TH:(half + 1) * TH, :] = res.results[core]["out"]
    return out
